# version 39
# baseline (speedup 1.0000x reference)
import contextlib
import numpy as np
import concourse.bass as bass
import concourse.mybir as mybir
from concourse.bass_utils import run_bass_kernel_spmd

F32 = mybir.dt.float32
BF16 = mybir.dt.bfloat16
AF = mybir.ActivationFunctionType
ALU = mybir.AluOpType
AX = mybir.AxisListType

NCORES = 8
B = 4
S = 4096
HID = 1024
NH = 16
TPC = S // NCORES
NTOK = B * TPC
KC = HID // 128
EPS = 1e-5
ENGINES = ("sync", "gpsimd", "vector", "scalar", "tensor")

TW_PERM = [4, 5, 6, 7, 0, 1, 2, 3]
TW_SIGN = [1.0, 1.0, -1.0, -1.0, -1.0, -1.0, 1.0, 1.0]
DEBUG_TAPS = False
DBG_U_MODE = "mul"


class Plan:

    def __init__(self):
        self.ops = {e: [] for e in ENGINES}
        self.counts = dict.fromkeys(
            ["ld", "gd", "st", "cc", "t", "v", "s", "g",
             "sg0", "sg1", "st0", "st1",
             "sh0", "sh1", "sh2", "sh3"], 0)

    def ev(self, sem):
        return (sem, self.counts[sem])

    def op(self, eng, fn, waits=(), inc=None, inc_n=1):
        waits = tuple(w for w in waits if w is not None)
        target = None
        if inc is not None:
            self.counts[inc] += inc_n
            target = self.counts[inc]
        self.ops[eng].append((waits, fn, inc, target, inc_n))
        return (inc, target) if inc else None


def build_program():
    nc = bass.Bass("TRN2", target_bir_lowering=False, debug=False)

    def din(name, shape):
        return nc.dram_tensor(name, shape, F32, kind="ExternalInput").ap()

    x4 = din("x4", [B, KC, 128, TPC])
    wq_d = din("wqT", [HID, HID])
    wv_d = din("wvT", [HID, HID])
    wkt = din("wkt", [8, 128, 8, 128])
    wot = din("wot2", [8, 128, 8, 128])
    cn1t = din("cn1t", [8, 128, 8, 128])
    cn2t = din("cn2t", [4, 128, 8, 128])
    cn3t = din("cn3t", [2, 128, 4, 128])
    cn4_d = din("cn4", [128, 2])
    gcw1t = din("gcw1t", [4, 128, 8, 128])
    gcw2t = din("gcw2t", [2, 128, 4, 128])
    gcw3_d = din("gcw3", [128, 2])
    wbeta_d = din("wbeta", [HID, NH])
    bias80_d = din("bbeta", [16, 1])
    lng_d = din("lng", [128, 8])
    lnb_d = din("lnb", [128, 8])
    bo_d = din("bo", [128, 8])
    cnb1_d = din("cnb1", [128, 8])
    cnb2_d = din("cnb2", [128, 4])
    cnb3_d = din("cnb3", [128, 2])
    gcb1_d = din("gcb1", [128, 4])
    gcb2_d = din("gcb2", [128, 2])
    misc_d = din("misc", [1, 4])
    ct_d = din("ct", [HID, B])
    wmod_d = din("wmod_sl", [8, 128, 768])
    ones128_d = din("ones128", [128, 128])
    ones1_d = din("ones1", [1, 128])
    selc_d = din("selc", [16, 1024])
    onehot_d = din("onehot", [128, 8])
    bsel_d = din("bsel", [128, 8, 16])
    ident_d = din("ident128", [128, 128])

    o4 = nc.dram_tensor("o4", [B, KC, 128, TPC], F32, kind="ExternalOutput").ap()

    warm_in = nc.dram_tensor("warm_in", [128, 4], F32)
    warm_out = nc.dram_tensor("warm_out", [NCORES * 128, 4], F32)
    modin_d = nc.dram_tensor("modin", [768, B], F32)
    modag_d = nc.dram_tensor("modag", [NCORES * 768, B], F32)
    agin_d = [nc.dram_tensor(f"agin{h}", [128, 32], F32) for h in range(2)]
    agout_d = [nc.dram_tensor(f"agout{h}", [NCORES * 128, 32], F32) for h in range(2)]
    arin_d = nc.dram_tensor("arin", [128, 32], F32)
    arout_d = nc.dram_tensor("arout", [128, 32], F32)

    stack = contextlib.ExitStack()

    def sbt(name, shape, dtype):
        return stack.enter_context(nc.sbuf_tensor(name, shape, dtype))

    xbf = sbt("xbf", [128, KC, NTOK], BF16)
    hy = sbt("hy", [128, KC, NTOK], BF16)
    pong = sbt("pong", [128, KC, NTOK], BF16)
    ud = sbt("ud", [128, KC, NTOK], BF16)
    pack = sbt("pack", [16, 3, NTOK], BF16)
    wring = sbt("wring", [128, 4, KC, 128], BF16)
    cn3b = sbt("cn3b", [128, 2, 4, 128], BF16)
    cn4b = sbt("cn4b", [128, 2, 1], BF16)
    stg = sbt("stg", [128, 2, 1024], F32)
    scr = sbt("scr", [128, 4, 512], F32)
    mubf = sbt("mubf", [128, B, 512], BF16)
    rstdbf = sbt("rstdbf", [128, B, 512], BF16)
    xsq = sbt("xsq", [128, 2, 512], BF16)
    nrm = sbt("nrm", [128, 4, 1024], BF16)
    hful = sbt("hful", [128, 2, 512], BF16)
    attno = sbt("attno", [128, 2, 512], BF16)
    twr = sbt("twr", [128, 2, 512], BF16)
    agin_s = sbt("agin_s", [128, 64], F32)
    gath = sbt("gath", [128, NCORES, 64], F32)
    h0s = sbt("h0s", [128, 32], F32)
    hpre = sbt("hpre", [128, 2, 32], F32)
    accp = sbt("accp", [128, 2, 32], F32)
    ptmp = sbt("ptmp", [128, 32], F32)
    hsum = sbt("hsum", [128, KC, B], F32)
    gctxf = sbt("gctxf", [128, KC, B], F32)
    gctxb = sbt("gctxb", [128, KC, B], BF16)
    g1b = sbt("g1b", [128, 4, B], BF16)
    g2b = sbt("g2b", [128, 2, B], BF16)
    gc3b = sbt("gc3b", [128, 2, 1], BF16)
    gfs = sbt("gfs", [1, 16], F32)
    tfrow = sbt("tfrow", [1, 2, 512], BF16)
    modp = sbt("modp", [128, 6, KC, B], F32)
    cts = sbt("cts", [128, KC, B], F32)
    css = sbt("css", [128, KC, B], F32)
    wsmall = sbt("wsmall", [128, KC, 80], BF16)
    bias80 = sbt("bbeta_s", [16, 1], F32)
    lng = sbt("lng_s", [128, KC], F32)
    lnb = sbt("lnb_s", [128, KC], F32)
    bo = sbt("bo_s", [128, KC], F32)
    cnb1 = sbt("cnb1_s", [128, KC], F32)
    cnb2 = sbt("cnb2_s", [128, 4], F32)
    cnb3 = sbt("cnb3_s", [128, 2], F32)
    gcb1 = sbt("gcb1_s", [128, 4], F32)
    gcb2 = sbt("gcb2_s", [128, 2], F32)
    misc = sbt("misc_s", [1, 4], F32)
    ones128 = sbt("ones128_s", [128, 128], BF16)
    ones1 = sbt("ones1_s", [1, 128], BF16)
    selc = sbt("selc_s", [16, 8, 128], BF16)
    onehot = sbt("onehot_s", [128, 8], F32)
    gsb = sbt("gsb", [16, B], BF16)
    lg2 = sbt("lg2", [128, KC, B], F32)
    lb2 = sbt("lb2", [128, KC, B], F32)
    gsbf = sbt("gsbf", [16, B], F32)
    bsel = sbt("bsel_s", [128, 8, 16], F32)
    ident = sbt("ident_s", [128, 16], F32)
    vsb = sbt("vsb", [128, 2, 512], BF16)
    ostg = sbt("ostg", [128, 2, 512], F32)

    pg = [stack.enter_context(nc.psum_tensor(f"pg{i}", [128, 512], F32))
          for i in range(8)]

    sems = {n: stack.enter_context(nc.semaphore("sem_" + n))
            for n in ["ld", "gd", "st", "cc", "t", "v", "s", "g",
                      "sg0", "sg1", "st0", "st1",
                      "sh0", "sh1", "sh2", "sh3"]}

    p = Plan()
    bank_free = [None] * 8
    vsb_free = [None, None]
    twr_free = [None, None]

    def bc(b):
        return slice(b * TPC, (b + 1) * TPC)

    e_wrm = p.op("gpsimd", lambda g: g.dma_start(
        out=warm_in.ap(), in_=ones128_d[:, 0:4]), inc="gd", inc_n=16)
    p.op("gpsimd", lambda g: g.collective_compute(
        "AllGather", ALU.bypass, replica_groups=[list(range(NCORES))],
        ins=[warm_in.ap().opt()], outs=[warm_out.ap().opt()]),
        waits=[e_wrm], inc="cc")
    lde = {}
    small_loads = {
        "bias80": (bias80[:, :], bias80_d),
        "lng": (lng[:, :], lng_d), "lnb": (lnb[:, :], lnb_d),
        "bo": (bo[:, :], bo_d), "cnb1": (cnb1[:, :], cnb1_d),
        "cnb2": (cnb2[:, :], cnb2_d), "cnb3": (cnb3[:, :], cnb3_d),
        "gcb1": (gcb1[:, :], gcb1_d), "gcb2": (gcb2[:, :], gcb2_d),
        "misc": (misc[:, :], misc_d), "onehot": (onehot[:, :], onehot_d),
        "cts": (cts[:, :, :], ct_d.rearrange("(kc p) b -> p kc b", p=128)),
        "bsel": (bsel[:, :, :], bsel_d),
        "ident": (ident[:, :], ident_d[:, 0:16]),
        "cn4": (cn4b[:, :, :], cn4_d),
    }
    del small_loads["cn4"]
    for name, (dst, src) in small_loads.items():
        lde_all = p.op("sync", lambda s_, d=dst, s2=src: s_.dma_start(out=d, in_=s2),
                       inc="ld", inc_n=16)
    for name in small_loads:
        lde[name] = lde_all

    stg_free = [None, None]
    stg_i = [0]

    def stg_slot():
        sl = stg_i[0] % 2
        stg_i[0] += 1
        return sl

    def load_stg(view_fn, src, queue="sync", free_ev_setter=True):
        sl = stg_slot()
        fr = stg_free[sl]
        fr = fr if isinstance(fr, list) else [fr]
        e = p.op(queue, lambda s_, sl=sl: s_.dma_start(out=view_fn(sl), in_=src),
                 waits=fr, inc=f"sg{sl}", inc_n=16)
        return sl, e

    sl, e = load_stg(lambda sl: stg[:, sl, 0:128], ones128_d)
    e_ones128 = p.op("gpsimd", lambda g, sl=sl: g.tensor_copy(
        ones128[:, :], stg[:, sl, 0:128]), waits=[e], inc="g")
    stg_free[sl] = e_ones128
    sl, e = load_stg(lambda sl: stg[0:1, sl, 0:128], ones1_d)
    e_ones1 = p.op("gpsimd", lambda g, sl=sl: g.tensor_copy(
        ones1[:, :], stg[0:1, sl, 0:128]), waits=[e], inc="g")
    stg_free[sl] = e_ones1
    sl, e = load_stg(lambda sl: stg[0:16, sl, 0:1024], selc_d)
    e_selc = p.op("gpsimd", lambda g, sl=sl: g.tensor_copy(
        selc[:, :, :], stg[0:16, sl, 0:1024].rearrange("k (c m) -> k c m", c=8)),
        waits=[e], inc="g")
    stg_free[sl] = e_selc
    sl, e = load_stg(lambda sl: stg[0:128, sl, 0:2], cn4_d)
    e_cn4 = p.op("gpsimd", lambda g, sl=sl: g.tensor_copy(
        cn4b[:, :, :], stg[:, sl, 0:2].rearrange("p (c o) -> p c o", o=1)),
        waits=[e], inc="g")
    stg_free[sl] = e_cn4

    e_wz = p.op("vector", lambda v: v.memset(wsmall[:, :, :], 0.0), inc="v")


    sl, e = load_stg(lambda sl: stg[:, sl, 0:128].rearrange("p (kc n) -> p kc n", kc=8),
                     wbeta_d.rearrange("(kc p) n -> p kc n", p=128))
    e_wbeta = p.op("gpsimd", lambda g, sl=sl: g.tensor_copy(
        wsmall[:, :, 64:80], stg[:, sl, 0:128].rearrange("p (kc n) -> p kc n", kc=8)),
        waits=[e, e_wz], inc="g")
    stg_free[sl] = e_wbeta

    wring_free = [None] * 4
    wring_i = [0]

    def load_weight(src, nkc, queue="sync"):
        wsl = wring_i[0] % 4
        wring_i[0] += 1
        sl, e = load_stg(
            lambda sl: stg[:, sl, 0:nkc * 128].rearrange("p (kc m) -> p kc m", kc=nkc),
            src, queue=queue)
        ev = p.op("gpsimd", lambda g, wsl=wsl, sl=sl, nkc=nkc: g.tensor_copy(
            wring[:, wsl, 0:nkc, :],
            stg[:, sl, 0:nkc * 128].rearrange("p (kc m) -> p kc m", kc=nkc)),
            waits=[e, wring_free[wsl]], inc="g")
        stg_free[sl] = ev
        return wsl, ev

    for mc3 in range(2):
        sl, e = load_stg(
            lambda sl: stg[:, sl, 0:512].rearrange("p (kc m) -> p kc m", kc=4),
            cn3t[mc3, :, :, :])
        e_cn3 = p.op("gpsimd", lambda g, mc3=mc3, sl=sl: g.tensor_copy(
            cn3b[:, mc3, :, :],
            stg[:, sl, 0:512].rearrange("p (kc m) -> p kc m", kc=4)),
            waits=[e], inc="g")
        stg_free[sl] = e_cn3

    e_silu = p.op("scalar", lambda s_: s_.activation(
        css[:, :, :], cts[:, :, :], AF.Silu), waits=[lde["cts"]], inc="s")
    e_modmm = {}
    for kc in range(KC):
        sl, e_ldm = load_stg(lambda sl: stg[:, sl, 0:768], wmod_d[kc, :, :], queue="scalar")
        for mf in range(6):
            ev = p.op("tensor", lambda t, kc=kc, mf=mf, sl=sl: t.matmul(
                pg[mf][:, 0:4],
                stg[:, sl, mf * 128:(mf + 1) * 128],
                css[:, kc, :],
                start=(kc == 0), stop=(kc == KC - 1)),
                waits=[e_ldm, e_silu, bank_free[mf]],
                inc="t")
            e_modmm[mf] = ev
        stg_free[sl] = ev
    e_mev = None
    for mf in range(6):
        e_mev = p.op("scalar", lambda s_, mf=mf: s_.copy(
            scr[:, 0, mf * 4:(mf + 1) * 4], pg[mf][:, 0:4]),
            waits=[e_modmm[mf]], inc="s")
        bank_free[mf] = e_mev

    e_x = {}
    half_free = [stg_free[0], stg_free[0], stg_free[1], stg_free[1]]
    for i, (b, kc) in enumerate([(b, kc) for b in range(B) for kc in range(KC)]):
        h = i % 4
        sl, off = h // 2, (h % 2) * 512
        xq = "sync" if h % 2 == 0 else "scalar"
        e = p.op(xq, lambda s_, sl=sl, off=off, b=b, kc=kc: s_.dma_start(
            out=stg[:, sl, off:off + 512], in_=x4[b, kc, :, :]),
            waits=[half_free[h]] if not isinstance(half_free[h], list)
            else half_free[h], inc=f"sh{h}", inc_n=16)
        e_x[(b, kc)] = p.op("vector", lambda v, b=b, kc=kc, sl=sl, off=off:
                            v.tensor_copy(
            xbf[:, kc, bc(b)], stg[:, sl, off:off + 512]), waits=[e], inc="v")
        half_free[h] = e_x[(b, kc)]
    stg_free[0] = [half_free[0], half_free[1]]
    stg_free[1] = [half_free[2], half_free[3]]


    e_mdma = p.op("gpsimd", lambda g: g.dma_start(
        out=modin_d.ap().rearrange("(mf p) b -> p mf b", p=128),
        in_=scr[:, 0, 0:24].rearrange("p (mf b) -> p mf b", b=B)),
        waits=[e_mev], inc="gd", inc_n=16)
    e_modcc = p.op("gpsimd", lambda g: g.collective_compute(
        "AllGather", ALU.bypass, replica_groups=[list(range(NCORES))],
        ins=[modin_d.ap().opt()], outs=[modag_d.ap().opt()]),
        waits=[e_mdma], inc="cc")
    e_modbk = p.op("gpsimd", lambda g: g.dma_start(
        out=modp[:, :, :, :].rearrange("p kind kc b -> p (kind kc) b"),
        in_=modag_d.ap().rearrange("(kind kc p) b -> p (kind kc) b", kind=6, p=128)),
        waits=[e_modcc], inc="gd", inc_n=16)
    e_scrfree = e_mdma

    e_ca1 = p.op("vector", lambda v: v.tensor_scalar_add(
        modp[:, 1, :, :], modp[:, 1, :, :], 1.0), waits=[e_modbk], inc="v")
    e_cb1 = p.op("vector", lambda v: v.tensor_scalar_add(
        modp[:, 4, :, :], modp[:, 4, :, :], 1.0), waits=[e_modbk], inc="v")
    e_clamp = {}
    for kind in (2, 5):
        def _v(i):
            return scr[:, i, 0:32].rearrange("p (kc b) -> p kc b", b=B)
        e_m1 = p.op("vector", lambda v, kind=kind: v.tensor_single_scalar(
            _v(1), modp[:, kind, :, :], 0.0, ALU.is_ge),
            waits=[e_modbk, e_scrfree, e_clamp.get(2)], inc="v")
        e_m2 = p.op("vector", lambda v, kind=kind: v.tensor_scalar_max(
            _v(2), modp[:, kind, :, :], 0.25), waits=[e_modbk, e_clamp.get(2)], inc="v")
        e_m3 = p.op("vector", lambda v, kind=kind: v.tensor_scalar_min(
            _v(3), modp[:, kind, :, :], -0.25), waits=[e_modbk, e_clamp.get(2)], inc="v")
        e_m4 = p.op("vector", lambda v: v.tensor_sub(_v(2), _v(2), _v(3)),
                    waits=[e_m2, e_m3], inc="v")
        e_m5 = p.op("vector", lambda v: v.tensor_mul(_v(1), _v(1), _v(2)),
                    waits=[e_m1, e_m4], inc="v")
        e_clamp[kind] = p.op("vector", lambda v, kind=kind: v.tensor_add(
            modp[:, kind, :, :], _v(1), _v(3)), waits=[e_m5], inc="v")
    e_mod = [e_ca1, e_cb1, e_clamp[2], e_clamp[5]]
    scr_free = [e_scrfree, None, e_clamp[5], e_clamp[5]]


    xsq_free = [None] * 2
    nrm_free = [None] * 4

    def ln_phase(src, dst, scale_col, bias_col, src_ev, pre=(), accum=None,
                 post=None, dst_free=None, aff_pre=(), wide_banks=False):
        out_ev = {}
        for b in range(B):
            mm_sum = None
            mm_sq = None
            for kc in range(KC):
                sl = (b * KC + kc) % 2
                sqeng, sqn = ("gpsimd", "g") if b % 2 == 0 else ("vector", "v")
                e_sq = p.op(
                    sqeng, lambda g, b=b, kc=kc, sl=sl: g.tensor_mul(
                        xsq[:, sl, :], src[:, kc, bc(b)], src[:, kc, bc(b)]),
                    waits=[src_ev.get((b, kc)) if src_ev else None,
                           xsq_free[sl], *pre], inc=sqn)
                nb = 2 * (b % 4) if wide_banks else 2 * (b % 2)
                ev = p.op("tensor", lambda t, b=b, kc=kc, nb=nb: t.matmul(
                    pg[nb][:, :], ones128[:, :], src[:, kc, bc(b)],
                    start=(kc == 0), stop=(kc == KC - 1)),
                    waits=[bank_free[nb], e_ones128,
                           src_ev.get((b, kc)) if src_ev else None, *pre],
                    inc="t" if kc == KC - 1 else None)
                if kc == KC - 1:
                    mm_sum = ev
                ev = p.op("tensor", lambda t, b=b, kc=kc, sl=sl, nb=nb: t.matmul(
                    pg[nb + 1][:, :], ones128[:, :], xsq[:, sl, :],
                    start=(kc == 0), stop=(kc == KC - 1)),
                    waits=[bank_free[nb + 1], e_sq], inc="t")
                xsq_free[sl] = ev
                if kc == KC - 1:
                    mm_sq = ev
            bs = 2 * (b % 4) if wide_banks else 2 * (b % 2)
            e_mu = p.op("vector", lambda v, b=b, bs=bs: v.tensor_scalar_mul(
                mubf[:, b, :], pg[bs][:, :], 1.0 / HID),
                waits=[mm_sum], inc="v")
            e_msq = p.op("vector", lambda v, b=b: v.tensor_mul(
                scr[:, b, :], mubf[:, b, :], mubf[:, b, :]),
                waits=[e_mu, scr_free[b]], inc="v")
            e_v1 = p.op("vector", lambda v, b=b, bs=bs: v.scalar_tensor_tensor(
                scr[:, b, :], pg[bs + 1][:, :], 1.0 / HID, scr[:, b, :],
                ALU.mult, ALU.subtract), waits=[mm_sq, e_msq], inc="v")
            e_v2 = p.op("vector", lambda v, b=b: v.tensor_scalar_add(
                scr[:, b, :], scr[:, b, :], EPS), waits=[e_v1], inc="v")
            e_rc = p.op("vector", lambda v, b=b: v.reciprocal(
                scr[:, b, :], scr[:, b, :]), waits=[e_v2], inc="v")
            e_rstd = p.op("scalar", lambda s_, b=b: s_.activation(
                rstdbf[:, b, :], scr[:, b, :], AF.Sqrt), waits=[e_rc], inc="s")
            scr_free[b] = e_rstd
            bank_free[bs] = e_mu
            bank_free[bs + 1] = e_v1
            for kc2 in range(0, KC, 2):
                sl = (b * 4 + kc2 // 2) % 4
                e_n1 = p.op("vector", lambda v, b=b, kc2=kc2, sl=sl: v.tensor_sub(
                    nrm[:, sl, :].rearrange("p (a t) -> p a t", a=2),
                    src[:, kc2:kc2 + 2, bc(b)],
                    mubf[:, b, :].rearrange("p (a t) -> p a t", a=1)
                    .broadcast_to([128, 2, TPC])),
                    waits=[e_mu, nrm_free[sl],
                           src_ev.get((b, kc2)) if src_ev else None,
                           src_ev.get((b, kc2 + 1)) if src_ev else None], inc="v")
                e_n2 = p.op("vector", lambda v, b=b, sl=sl: v.tensor_mul(
                    nrm[:, sl, :].rearrange("p (a t) -> p a t", a=2),
                    nrm[:, sl, :].rearrange("p (a t) -> p a t", a=2),
                    rstdbf[:, b, :].rearrange("p (a t) -> p a t", a=1)
                    .broadcast_to([128, 2, TPC])),
                    waits=[e_n1, e_rstd], inc="v")
                for kk in range(2):
                    kc = kc2 + kk
                    acc_ap = accum(kc, b) if accum else None
                    dfree = dst_free(kc, b) if dst_free else None
                    nv = nrm[:, sl, kk * TPC:(kk + 1) * TPC]
                    if acc_ap is not None:
                        e_aff = p.op("scalar", lambda s_, b=b, kc=kc, nv=nv,
                                     a=acc_ap: s_.activation(
                                         dst(kc, b), nv, AF.Identity,
                                         bias=bias_col(kc, b),
                                         scale=scale_col(kc, b),
                                         accum_out=a),
                                     waits=[e_n2, dfree, *aff_pre], inc="s")
                    else:
                        e_aff = p.op("scalar", lambda s_, b=b, kc=kc, nv=nv:
                                     s_.activation(
                                         dst(kc, b), nv, AF.Identity,
                                         bias=bias_col(kc, b),
                                         scale=scale_col(kc, b)),
                                     waits=[e_n2, dfree, *aff_pre], inc="s")
                    out_ev[(b, kc)] = e_aff
                    if post is not None:
                        out_ev[(b, kc)] = post(kc, b, e_aff)
                nrm_free[sl] = out_ev[(b, kc2 + 1)]
        return out_ev

    e_h1 = ln_phase(
        src=xbf, dst=lambda kc, b: hy[:, kc, bc(b)],
        scale_col=lambda kc, b: modp[:, 1, kc, b:b + 1],
        bias_col=lambda kc, b: modp[:, 0, kc, b:b + 1],
        src_ev=e_x, aff_pre=e_mod)

    e_cs = {0: None, 32: None}
    for wd, col0 in ((wq_d, 0), (wv_d, 32)):
        cs_last = {6: None, 7: None}
        for qc in range(KC):
            sl, e_wc = load_stg(lambda sl, wd=wd, qc=qc: stg[:, sl, :],
                                wd[qc * 128:(qc + 1) * 128, :], queue="scalar")
            for half, bank in ((0, 6), (1, 7)):
                cs_last[bank] = p.op("tensor", lambda t, qc=qc, half=half,
                                     bank=bank, sl=sl: t.matmul(
                    pg[bank][0:16, :],
                    bsel[:, qc, :],
                    stg[:, sl, half * 512:(half + 1) * 512],
                    start=(qc == 0), stop=(qc == KC - 1)),
                    waits=[e_wc, lde["bsel"], bank_free[bank]], inc="t")
            stg_free[sl] = cs_last[7]
        e_ev0 = p.op("vector", lambda v: v.tensor_copy(
            scr[0:16, 0, :], pg[6][0:16, :]),
            waits=[cs_last[6], e_scrfree, e_clamp[5]], inc="v")
        e_ev1 = p.op("vector", lambda v: v.tensor_copy(
            scr[0:16, 1, :], pg[7][0:16, :]),
            waits=[cs_last[7], e_scrfree, e_clamp[5]], inc="v")
        bank_free[6] = e_ev0
        bank_free[7] = e_ev1
        last = None
        for kc in range(KC):
            bank = 6 + kc % 2
            e_tr = p.op("tensor", lambda t, kc=kc, bank=bank: t.transpose(
                pg[bank][:, 0:16],
                scr[0:16, kc // 4, (kc % 4) * 128:(kc % 4) * 128 + 128],
                ident[0:16, 0:16]),
                waits=[e_ev0 if bank == 6 else e_ev1, lde["ident"],
                       bank_free[bank]], inc="t")
            last = p.op("vector", lambda v, kc=kc, bank=bank, col0=col0:
                        v.tensor_copy(
                            wsmall[:, kc, col0:col0 + 16], pg[bank][:, 0:16]),
                        waits=[e_tr, e_wz], inc="v")
            bank_free[bank] = last
        e_cs[col0] = last
    e_wq_cs = e_cs[0]
    e_wv_cs = e_cs[32]
    scr_free[0] = e_cs[32]
    scr_free[1] = e_cs[32]
    e_wsmall = [e_wq_cs, e_wv_cs, e_wbeta]

    e_pack = {}
    for b in range(B):
        bank = 4 + (b % 2)
        mm_last = None
        for kc in range(KC):
            mm_last = p.op("tensor", lambda t, b=b, kc=kc, bank=bank: t.matmul(
                pg[bank][0:80, :], wsmall[:, kc, :], hy[:, kc, bc(b)],
                start=(kc == 0), stop=(kc == KC - 1)),
                waits=[bank_free[bank], e_h1[(b, kc)], *e_wsmall], inc="t")
        eq = p.op("vector", lambda v, b=b, bank=bank: v.tensor_copy(
            pack[:, 0, bc(b)], pg[bank][0:16, :]), waits=[mm_last], inc="v")
        ev2 = p.op("vector", lambda v, b=b, bank=bank: v.tensor_copy(
            pack[:, 1, bc(b)], pg[bank][32:48, :]), waits=[mm_last], inc="v")
        ez = p.op("scalar", lambda s_, b=b, bank=bank: s_.activation(
            pack[:, 2, bc(b)], pg[bank][64:80, :], AF.Sigmoid,
            bias=bias80[:, 0:1]), waits=[mm_last, lde["bias80"]], inc="s")
        bank_free[bank] = ez
        bank_free[bank] = p.op("vector", lambda v: v.engine_nop(),
                               waits=[eq, ev2, ez], inc="v")
        e_pack[(b, "q")] = eq
        e_pack[(b, "v")] = ev2
        e_pack[(b, "z")] = ez

    e_u = {}
    for c in range(KC):
        wsl, e_wk = load_weight(wkt[c, :, :, :], KC)
        mm_k = {}
        for kcin in range(KC):
            for b in range(B):
                ev = p.op("tensor", lambda t, c=c, kcin=kcin, b=b, wsl=wsl: t.matmul(
                    pg[b][:, :], wring[:, wsl, kcin, :], hy[:, kcin, bc(b)],
                    start=(kcin == 0), stop=(kcin == KC - 1)),
                    waits=[e_wk, bank_free[b], e_h1[(b, kcin)]],
                    inc="t" if kcin == KC - 1 else None)
                if kcin == KC - 1:
                    mm_k[b] = ev
        wring_free[wsl] = p.ev("t")
        for b in range(B):
            vb = 4 + (b % 2)
            vsl = b % 2
            e_vs = p.op("tensor", lambda t, c=c, b=b, vb=vb: t.matmul(
                pg[vb][:, :], selc[:, c, :], pack[:, 1, bc(b)],
                start=True, stop=True),
                waits=[bank_free[vb], e_selc, e_pack[(b, "v")]], inc="t")
            e_ve = p.op("vector", lambda v, vb=vb, vsl=vsl: v.tensor_copy(
                vsb[:, vsl, :], pg[vb][:, :]),
                waits=[e_vs, vsb_free[vsl]], inc="v")
            bank_free[vb] = e_ve
            if DBG_U_MODE == "k":
                e_u[(b, c)] = p.op("vector", lambda v, c=c, b=b: v.tensor_copy(
                    ud[:, c, bc(b)], pg[b][:, :]),
                    waits=[mm_k[b], e_ve], inc="v")
            elif DBG_U_MODE == "vs":
                e_u[(b, c)] = p.op("vector", lambda v, c=c, b=b, vsl=vsl: v.tensor_copy(
                    ud[:, c, bc(b)], vsb[:, vsl, :]),
                    waits=[mm_k[b], e_ve], inc="v")
            else:
                e_u[(b, c)] = p.op("vector", lambda v, c=c, b=b, vsl=vsl: v.tensor_mul(
                    ud[:, c, bc(b)], pg[b][:, :], vsb[:, vsl, :]),
                    waits=[mm_k[b], e_ve], inc="v")
            vsb_free[vsl] = e_u[(b, c)]
            bank_free[b] = e_u[(b, c)]

    dbg_ev = {}

    def dbg_tap(name, ap, shape, dtype, waits):
        if not DEBUG_TAPS:
            return None
        dst = nc.dram_tensor(name, shape, dtype, kind="ExternalOutput").ap()
        ev = p.op("gpsimd", lambda g, dst=dst, ap=ap: g.dma_start(out=dst, in_=ap),
                  waits=waits, inc="gd", inc_n=16)
        dbg_ev[name] = ev
        return ev

    e_tap_u = dbg_tap("d_u", ud[:, :, :].rearrange("p a t -> p (a t)"),
                      [128, KC * NTOK], BF16,
                      [e_u[(b, c)] for b in range(B) for c in range(KC)])

    wo_e = {}
    for oc in range(KC):
        sl, e = load_stg(lambda sl, oc=oc: stg[:, sl, 0:1024], wot[oc, :, :, :]
                         .rearrange("p mc m -> p (mc m)"))
        wo_e[oc] = p.op("gpsimd", lambda g, oc=oc, sl=sl: g.tensor_copy(
            pong[:, oc, 0:1024], stg[:, sl, 0:1024]), waits=[e], inc="g")
        stg_free[sl] = wo_e[oc]

    e_v1col = {}
    for i, (b, c) in enumerate([(b, c) for b in range(B) for c in range(KC)]):
        bank = 4 + (i % 2)
        ssl = i % 4
        e_bb = p.op("tensor", lambda t, b=b, c=c, bank=bank: t.matmul(
            pg[bank][:, :], selc[:, c, :], pack[:, 2, bc(b)],
            start=True, stop=True),
            waits=[bank_free[bank], e_selc, e_pack[(b, "z")]], inc="t")
        e_s1 = p.op("vector", lambda v, b=b, c=c, bank=bank, ssl=ssl:
                    v.tensor_tensor_scan(
                        scr[:, ssl, :], pg[bank][:, :], ud[:, c, bc(b)],
                        0.0, ALU.mult, ALU.add),
                    waits=[e_bb, e_u[(b, c)], scr_free[ssl]], inc="v")
        bank_free[bank] = e_s1
        vcol = (b // 2) * 32 + (b % 2) * KC + c
        e_vc = p.op("vector", lambda v, vcol=vcol, ssl=ssl: v.tensor_copy(
            agin_s[:, vcol:vcol + 1], scr[:, ssl, 511:512]),
            waits=[e_s1], inc="v")
        scr_free[ssl] = e_vc
        e_v1col[(b, c)] = e_vc
    e_g = {}
    e_gz = p.op("vector", lambda v: v.memset(twr[0:16, 1, :], 0.0), inc="v")
    for b in range(B):
        eg1 = p.op("vector", lambda v, b=b: v.tensor_tensor_scan(
            twr[0:16, 0, :], pack[:, 2, bc(b)], twr[0:16, 1, :], 1.0,
            ALU.mult, ALU.add),
            waits=[e_pack[(b, "z")], e_gz, e_g.get(b - 1)], inc="v")
        e_g[b] = p.op("vector", lambda v, b=b: v.tensor_copy(
            gsb[:, b:b + 1], twr[0:16, 0, 511:512]), waits=[eg1], inc="v")
    twr_free[0] = e_g[B - 1]
    twr_free[1] = e_g[B - 1]
    e_h0 = {}
    for half in range(2):
        bs = (0, 1) if half == 0 else (2, 3)
        e_gexp_mm = None
        for b in bs:
            for c in range(KC):
                e_gexp_mm = p.op("tensor", lambda t, b=b, c=c: t.matmul(
                    pg[6][:, (b % 2) * KC + c:(b % 2) * KC + c + 1],
                    selc[:, c, :], gsb[:, b:b + 1],
                    start=True, stop=True),
                    waits=[bank_free[6], e_selc, e_g[b]], inc="t")
        e_gex = p.op("vector", lambda v, half=half: v.tensor_copy(
            agin_s[:, half * 32 + 16:half * 32 + 32], pg[6][:, 0:16]),
            waits=[e_gexp_mm], inc="v")
        bank_free[6] = e_gex

        half_v1 = [e_v1col[(b, c)] for b in bs for c in range(KC)] + [e_gex]
        e_agdma = p.op("gpsimd", lambda g, half=half: g.dma_start(
            out=agin_d[half].ap(), in_=agin_s[:, half * 32:half * 32 + 32]),
            waits=half_v1, inc="gd", inc_n=16)
        e_agcc = p.op("gpsimd", lambda g, half=half: g.collective_compute(
            "AllGather", ALU.bypass, replica_groups=[list(range(NCORES))],
            ins=[agin_d[half].ap().opt()], outs=[agout_d[half].ap().opt()]),
            waits=[e_agdma], inc="cc")
        e_agbk = p.op("gpsimd", lambda g, half=half: g.dma_start(
            out=gath[:, :, half * 32:half * 32 + 32],
            in_=agout_d[half].ap().rearrange("(r p) c -> p r c", p=128)),
            waits=[e_agcc], inc="gd", inc_n=16)
        hp = half % 2
        e_z0 = p.op("vector", lambda v, hp=hp: v.memset(hpre[:, hp, 0:16], 0.0),
                    inc="v")
        e_z1 = p.op("vector", lambda v, hp=hp: v.memset(accp[:, hp, 16:32], 0.0),
                    inc="v")
        e_hp = {0: e_z0}
        e_ac = {-1: e_z1}
        for r in range(NCORES):
            cur, nxt = r % 2, (r + 1) % 2
            e_ac[r] = p.op("vector", lambda v, r=r, cur=cur, nxt=nxt, hp=hp,
                           half=half: v.scalar_tensor_tensor(
                               accp[:, hp, cur * 16:cur * 16 + 16],
                               hpre[:, hp, cur * 16 - cur * 16:16] if False else
                               hpre[:, hp, 0:16] if cur == 0 else
                               hpre[:, hp, 16:32],
                               onehot[:, r:r + 1],
                               accp[:, hp, nxt * 16:nxt * 16 + 16],
                               ALU.mult, ALU.add),
                           waits=[e_hp[r], e_ac.get(r - 1, e_z1), lde["onehot"],
                                  e_agbk], inc="v")
            if r < NCORES - 1:
                e_t = p.op("vector", lambda v, r=r, cur=cur, hp=hp, half=half:
                           v.tensor_mul(
                               ptmp[:, 0:16],
                               gath[:, r, half * 32 + 16:half * 32 + 32],
                               hpre[:, hp, 0:16] if cur == 0 else
                               hpre[:, hp, 16:32]),
                           waits=[e_hp[r], e_agbk], inc="v")
                e_hp[r + 1] = p.op("vector", lambda v, r=r, nxt=nxt, hp=hp,
                                   half=half: v.tensor_add(
                                       hpre[:, hp, 0:16] if nxt == 0 else
                                       hpre[:, hp, 16:32],
                                       ptmp[:, 0:16],
                                       gath[:, r, half * 32:half * 32 + 16]),
                                   waits=[e_t], inc="v")
        e_h0[half] = p.op("vector", lambda v, hp=hp, half=half: v.tensor_copy(
            h0s[:, half * 16:half * 16 + 16],
            accp[:, hp, ((NCORES - 1) % 2) * 16:((NCORES - 1) % 2) * 16 + 16]),
            waits=[e_ac[NCORES - 1]], inc="v")

    e_o = {}
    for i, (b, c) in enumerate([(b, c) for b in range(B) for c in range(KC)]):
        bank = 4 + (i % 2)
        qb = 6 + (i % 2)
        hsl = i % 2
        e_b2 = p.op("tensor", lambda t, b=b, c=c, bank=bank: t.matmul(
            pg[bank][:, :], selc[:, c, :], pack[:, 2, bc(b)],
            start=True, stop=True),
            waits=[bank_free[bank], e_pack[(b, "z")]], inc="t")
        e_q2 = p.op("tensor", lambda t, b=b, c=c, qb=qb: t.matmul(
            pg[qb][:, :], selc[:, c, :], pack[:, 0, bc(b)],
            start=True, stop=True),
            waits=[bank_free[qb], e_pack[(b, "q")]], inc="t")
        e_s2 = p.op("vector", lambda v, b=b, c=c, bank=bank, hsl=hsl:
                    v.tensor_tensor_scan(
                        hful[:, hsl, :], pg[bank][:, :], ud[:, c, bc(b)],
                        h0s[:, b * KC + c:b * KC + c + 1], ALU.mult, ALU.add),
                    waits=[e_b2, e_u[(b, c)], e_h0[b // 2], e_tap_u,
                           e_o.get(((i - 2) // KC, (i - 2) % KC)) if i >= 2 else None],
                    inc="v")
        bank_free[bank] = e_s2
        e_o[(b, c)] = p.op("vector", lambda v, b=b, c=c, qb=qb, hsl=hsl:
                           v.tensor_mul(
                               ud[:, c, bc(b)], hful[:, hsl, :], pg[qb][:, :]),
                           waits=[e_s2, e_q2], inc="v")
        bank_free[qb] = e_o[(b, c)]

    hful_free = [e_o[(B - 1, KC - 2)], e_o[(B - 1, KC - 1)]]
    ytmp = nrm
    ytmp_free = [nrm_free[0], nrm_free[1], nrm_free[2], nrm_free[3]]

    e_y = {}
    for b in range(B):
        for mc in range(KC):
            bank = mc % 4
            mm_o = None
            for oc in range(KC):
                mm_o = p.op("tensor", lambda t, b=b, mc=mc, oc=oc, bank=bank:
                            t.matmul(
                                pg[bank][:, :],
                                pong[:, oc, mc * 128:(mc + 1) * 128],
                                ud[:, oc, bc(b)],
                                start=(oc == 0), stop=(oc == KC - 1)),
                            waits=[wo_e[oc], bank_free[bank], e_o[(b, oc)]],
                            inc="t" if oc == KC - 1 else None)
            ysl = (b * KC + mc) % 4
            e_yt = p.op("scalar", lambda s_, mc=mc, b=b, bank=bank, ysl=ysl:
                        s_.activation(
                            ytmp[:, ysl, 0:TPC], pg[bank][:, :], AF.Identity,
                            bias=bo[:, mc:mc + 1]),
                        waits=[mm_o, lde["bo"], ytmp_free[ysl]], inc="s")
            bank_free[bank] = e_yt
            yeng, yinc = ("vector", "v") if ysl % 2 == 0 else ("gpsimd", "g")
            e_y[(b, mc)] = p.op(yeng, lambda v, mc=mc, b=b, ysl=ysl:
                                v.tensor_add(
                                    hy[:, mc, bc(b)], hy[:, mc, bc(b)],
                                    ytmp[:, ysl, 0:TPC]),
                                waits=[e_yt, e_h1[(b, mc)]], inc=yinc)
            ytmp_free[ysl] = e_y[(b, mc)]

    e_tap_o = dbg_tap("d_o", ud[:, :, :].rearrange("p a t -> p (a t)"),
                      [128, KC * NTOK], BF16,
                      [e_o[(b, c)] for b in range(B) for c in range(KC)])
    e_tap_y = dbg_tap("d_y", hy[:, :, :].rearrange("p a t -> p (a t)"),
                      [128, KC * NTOK], BF16,
                      [e_y[(b, kc)] for b in range(B) for kc in range(KC)])

    for _i in range(4):
        nrm_free[_i] = ytmp_free[_i]
    attno_free = [None, None]
    e_x2 = {}

    def post_attn(kc, b, e_aff):
        sl = (b * KC + kc) % 2
        ev = p.op("vector", lambda v, kc=kc, b=b, sl=sl: v.tensor_add(
            xbf[:, kc, bc(b)], xbf[:, kc, bc(b)], attno[:, sl, :]),
            waits=[e_aff], inc="v")
        attno_free[sl] = ev
        e_x2[(b, kc)] = ev
        return ev

    e_y_src = {(b, kc): e_y[(b, kc)] for b in range(B) for kc in range(KC)}

    e_lg2 = p.op("vector", lambda v: v.tensor_mul(
        lg2[:, :, :], lng[:, :].rearrange("p (kc o) -> p kc o", o=1)
        .broadcast_to([128, KC, B]), modp[:, 2, :, :]),
        waits=[lde["lng"], e_clamp[2]], inc="v")
    e_lb2 = p.op("vector", lambda v: v.tensor_mul(
        lb2[:, :, :], lnb[:, :].rearrange("p (kc o) -> p kc o", o=1)
        .broadcast_to([128, KC, B]), modp[:, 2, :, :]),
        waits=[lde["lnb"], e_clamp[2]], inc="v")

    def attno_dst(kc, b):
        return attno[:, (b * KC + kc) % 2, :]

    ln_phase(
        src=hy, dst=attno_dst,
        scale_col=lambda kc, b: lg2[:, kc, b:b + 1],
        bias_col=lambda kc, b: lb2[:, kc, b:b + 1],
        src_ev=e_y_src, aff_pre=[e_lg2, e_lb2],
        post=post_attn,
        dst_free=lambda kc, b: attno_free[(b * KC + kc) % 2],
        wide_banks=True)

    e_h2 = ln_phase(
        src=xbf, dst=lambda kc, b: hy[:, kc, bc(b)],
        scale_col=lambda kc, b: modp[:, 4, kc, b:b + 1],
        bias_col=lambda kc, b: modp[:, 3, kc, b:b + 1],
        src_ev=e_x2, pre=[e_tap_y] if DEBUG_TAPS else [],
        aff_pre=[e_cb1], accum=lambda kc, b: hsum[:, kc, b:b + 1],
        wide_banks=True)

    gc1_pre = {}
    for mc in range(4):
        gc1_pre[mc] = load_weight(gcw1t[mc, :, :, :], KC)
    gc2_pre = {}
    for mc2 in range(2):
        gc2_pre[mc2] = load_weight(gcw2t[mc2, :, :, :], 4)
    sl, e = load_stg(lambda sl: stg[0:128, sl, 0:2], gcw3_d)
    e_gc3 = p.op("gpsimd", lambda g, sl=sl: g.tensor_copy(
        gc3b[:, :, :], stg[:, sl, 0:2].rearrange("p (c o) -> p c o", o=1)),
        waits=[e], inc="g")
    stg_free[sl] = e_gc3

    cn1_pre = {}
    for mc in range(4):
        cn1_pre[mc] = load_weight(cn1t[mc, :, :, :], KC)

    all_h2 = [e_h2[(b, kc)] for b in range(B) for kc in range(KC)]
    e_ardma = p.op("gpsimd", lambda g: g.dma_start(
        out=arin_d.ap(), in_=hsum[:, :, :].rearrange("p kc b -> p (kc b)")),
        waits=all_h2, inc="gd", inc_n=16)
    e_arcc = p.op("gpsimd", lambda g: g.collective_compute(
        "AllReduce", ALU.add, replica_groups=[list(range(NCORES))],
        ins=[arin_d.ap().opt()], outs=[arout_d.ap().opt()]),
        waits=[e_ardma], inc="cc")
    e_arbk = p.op("gpsimd", lambda g: g.dma_start(
        out=gctxf[:, :, :].rearrange("p kc b -> p (kc b)"), in_=arout_d.ap()),
        waits=[e_arcc], inc="gd", inc_n=16)
    e_gctx = p.op("vector", lambda v: v.tensor_scalar_mul(
        gctxf[:, :, :], gctxf[:, :, :], 1.0 / S), waits=[e_arbk], inc="v")



    e_tap_h2 = dbg_tap("d_h2", hy[:, :, :].rearrange("p a t -> p (a t)"),
                       [128, KC * NTOK], BF16,
                       [e_h2[(b, kc)] for b in range(B) for kc in range(KC)])

    e_coupconst = []

    def plan_gf():
        e_gcb = p.op("vector", lambda v: v.tensor_copy(
            gctxb[:, :, :], gctxf[:, :, :]), waits=[e_gctx], inc="v")
        e_g1mm = {}
        for mc in range(4):
            wsl, e_l = gc1_pre[mc]
            for kc in range(KC):
                e_g1mm[mc] = p.op("tensor", lambda t, kc=kc, mc=mc, wsl=wsl:
                                  t.matmul(
                    pg[4 + mc][:, 0:4], wring[:, wsl, kc, :], gctxb[:, kc, :],
                    start=(kc == 0), stop=(kc == KC - 1)),
                    waits=[e_l, e_gcb, bank_free[4 + mc]], inc="t")
            wring_free[wsl] = p.ev("t")
        e_g1 = {}
        for mc in range(4):
            e_g1[mc] = p.op("scalar", lambda s_, mc=mc: s_.activation(
                g1b[:, mc, :], pg[4 + mc][:, 0:4], AF.Gelu,
                bias=gcb1[:, mc:mc + 1]), waits=[e_g1mm[mc], lde["gcb1"]], inc="s")
            bank_free[4 + mc] = e_g1[mc]
        e_g2mm = {}
        for mc2 in range(2):
            wsl, e_l = gc2_pre[mc2]
            for kc2 in range(4):
                e_g2mm[mc2] = p.op("tensor", lambda t, kc2=kc2, mc2=mc2, wsl=wsl:
                                   t.matmul(
                    pg[mc2][:, 0:4], wring[:, wsl, kc2, :], g1b[:, kc2, :],
                    start=(kc2 == 0), stop=(kc2 == 3)),
                    waits=[e_l, e_g1[kc2], bank_free[mc2]], inc="t")
            wring_free[wsl] = p.ev("t")
        e_g2 = {}
        for mc2 in range(2):
            e_g2[mc2] = p.op("scalar", lambda s_, mc2=mc2: s_.activation(
                g2b[:, mc2, :], pg[mc2][:, 0:4], AF.Gelu,
                bias=gcb2[:, mc2:mc2 + 1]), waits=[e_g2mm[mc2], lde["gcb2"]], inc="s")
            bank_free[mc2] = e_g2[mc2]
        e_g3mm = None
        for kc3 in range(2):
            e_g3mm = p.op("tensor", lambda t, kc3=kc3: t.matmul(
                pg[2][0:1, 0:4], gc3b[:, kc3, :], g2b[:, kc3, :],
                start=(kc3 == 0), stop=(kc3 == 1)),
                waits=[e_gc3, e_g2[kc3], bank_free[2]], inc="t")
        e_gf = p.op("scalar", lambda s_: s_.activation(
            gfs[0:1, 0:4], pg[2][0:1, 0:4], AF.Sigmoid,
            bias=misc[0:1, 1:2]), waits=[e_g3mm, lde["misc"]], inc="s")
        bank_free[2] = e_gf
        e_cb = p.op("vector", lambda v: v.tensor_scalar(
            gfs[0:1, 4:8], gfs[0:1, 0:4], 1.4, -1.0, ALU.mult, ALU.add),
            waits=[e_gf], inc="v")
        e_cb2 = p.op("vector", lambda v: v.tensor_scalar(
            gfs[0:1, 4:8], gfs[0:1, 4:8], misc[0:1, 2:3], 0.1, ALU.mult, ALU.add),
            waits=[e_cb], inc="v")
        e_sc = p.op("vector", lambda v: v.tensor_scalar_mul(
            gfs[0:1, 8:9], misc[0:1, 2:3], 0.6), waits=[e_gf], inc="v")
        e_coupconst.extend([e_cb2, e_sc])

    cur_src, cur_src_ev = hy, e_h2
    tfrow_free = [None, None]
    ostg_free = [None, None]
    csb_free = [attno_free[0], attno_free[1], hful_free[0], hful_free[1]]
    for cyc in range(3):
        OUT = pong if cyc % 2 == 0 else hy
        IN = cur_src
        in_ev = cur_src_ev
        out_ev = {}
        e_ha = {}
        for mc in range(KC):
            if cyc == 0 and mc < 4:
                wsl, e_w = cn1_pre[mc]
            else:
                wsl, e_w = load_weight(cn1t[mc, :, :, :], KC)
            mm = {}
            for kc in range(KC):
                for b in range(B):
                    hab = b + 4 * (mc % 2)
                    ev = p.op("tensor", lambda t, mc=mc, kc=kc, b=b, wsl=wsl,
                              hab=hab: t.matmul(
                                  pg[hab][:, :], wring[:, wsl, kc, :],
                                  IN[:, kc, bc(b)],
                                  start=(kc == 0), stop=(kc == KC - 1)),
                              waits=[e_w, bank_free[hab], in_ev.get((b, kc))],
                              inc="t" if kc == KC - 1 else None)
                    if kc == KC - 1:
                        mm[b] = ev
            wring_free[wsl] = p.ev("t")
            for b in range(B):
                hab = b + 4 * (mc % 2)
                e_ha[(mc, b)] = p.op("scalar", lambda s_, mc=mc, b=b, hab=hab:
                                     s_.activation(
                                         ud[:, mc, bc(b)], pg[hab][:, :], AF.Gelu,
                                         bias=cnb1[:, mc:mc + 1]),
                                     waits=[mm[b], lde["cnb1"],
                                            e_tap_o if cyc == 0 else None,
                                            ud_free.get((mc, b)) if cyc > 0 else None],
                                     inc="s")
                bank_free[hab] = e_ha[(mc, b)]
        ud_free = {}
        e_hb = {}
        for mc2 in range(4):
            wsl, e_w = load_weight(cn2t[mc2, :, :, :], KC)
            mm = {}
            for kc in range(KC):
                for b in range(B):
                    hbb = b + 4 * (mc2 % 2)
                    ev = p.op("tensor", lambda t, mc2=mc2, kc=kc, b=b, wsl=wsl,
                              hbb=hbb: t.matmul(
                                  pg[hbb][:, :], wring[:, wsl, kc, :],
                                  ud[:, kc, bc(b)],
                                  start=(kc == 0), stop=(kc == KC - 1)),
                              waits=[e_w, bank_free[hbb], e_ha[(kc, b)]],
                              inc="t" if kc == KC - 1 else None)
                    if kc == KC - 1:
                        mm[b] = ev
                    ud_free[(kc, b)] = p.ev("t")
            wring_free[wsl] = p.ev("t")
            for b in range(B):
                hbb = b + 4 * (mc2 % 2)
                e_hb[(mc2, b)] = p.op("scalar", lambda s_, mc2=mc2, b=b, OUT=OUT,
                                      hbb=hbb: s_.activation(
                                          OUT[:, 4 + mc2, bc(b)], pg[hbb][:, :],
                                          AF.Gelu, bias=cnb2[:, mc2:mc2 + 1]),
                                      waits=[mm[b], lde["cnb2"],
                                             e_tap_h2 if OUT is hy else None],
                                      inc="s")
                bank_free[hbb] = e_hb[(mc2, b)]
        e_hc = {}
        for mc3 in range(2):
            mm = {}
            for kc2 in range(4):
                for b in range(B):
                    bank = 4 * mc3 + b
                    ev = p.op("tensor", lambda t, mc3=mc3, kc2=kc2, b=b, OUT=OUT:
                              t.matmul(
                                  pg[4 * mc3 + b][:, :], cn3b[:, mc3, kc2, :],
                                  OUT[:, 4 + kc2, bc(b)],
                                  start=(kc2 == 0), stop=(kc2 == 3)),
                              waits=[e_cn3, bank_free[4 * mc3 + b],
                                     e_hb[(kc2, b)]],
                              inc="t" if kc2 == 3 else None)
                    if kc2 == 3:
                        mm[b] = ev
            for b in range(B):
                e_hc[(mc3, b)] = p.op("scalar", lambda s_, mc3=mc3, b=b, OUT=OUT:
                                      s_.activation(
                                          OUT[:, mc3, bc(b)], pg[4 * mc3 + b][:, :],
                                          AF.Gelu, bias=cnb3[:, mc3:mc3 + 1]),
                                      waits=[mm[b], lde["cnb3"]], inc="s")
                bank_free[4 * mc3 + b] = e_hc[(mc3, b)]
        if cyc == 0:
            plan_gf()
        e_coup = {}
        for b in range(B):
            mm_tf = None
            for kc4 in range(2):
                mm_tf = p.op("tensor", lambda t, b=b, kc4=kc4, OUT=OUT: t.matmul(
                    pg[b][0:1, :], cn4b[:, kc4, :], OUT[:, kc4, bc(b)],
                    start=(kc4 == 0), stop=(kc4 == 1)),
                    waits=[e_cn4, bank_free[b], e_hc[(kc4, b)]], inc="t")
            tsl = b % 2
            e_tf = p.op("scalar", lambda s_, b=b, tsl=tsl: s_.activation(
                tfrow[0:1, tsl, :], pg[b][0:1, :], AF.Sigmoid,
                bias=misc[0:1, 0:1]), waits=[mm_tf, lde["misc"],
                                             tfrow_free[tsl]], inc="s")
            bank_free[b] = e_tf
            e_cr = p.op("vector", lambda v, b=b, tsl=tsl: v.tensor_scalar(
                tfrow[0:1, tsl, :], tfrow[0:1, tsl, :],
                gfs[0:1, 8:9], gfs[0:1, 4 + b:5 + b], ALU.mult, ALU.add),
                waits=[e_tf, *e_coupconst], inc="v")
            cpb = 4 + b
            e_coup[b] = p.op("tensor", lambda t, b=b, tsl=tsl, cpb=cpb: t.matmul(
                pg[cpb][:, :], ones1[:, :], tfrow[0:1, tsl, :],
                start=True, stop=True),
                waits=[e_ones1, e_cr, bank_free[cpb]], inc="t")
            tfrow_free[tsl] = e_coup[b]
        csb = [attno[:, 0, :], attno[:, 1, :], hful[:, 0, :], hful[:, 1, :]]
        e_csb = {}
        for b in range(B):
            e_csb[b] = p.op("scalar", lambda s_, b=b: s_.copy(
                csb[b], pg[4 + b][:, :]),
                waits=[e_coup[b], csb_free[b]], inc="s")
            bank_free[4 + b] = e_csb[b]
        for c in range(KC):
            for b in range(B):
                wsl2 = (c * B + b) % 2
                e_tw = p.op("vector", lambda v, b=b, c=c, wsl2=wsl2, IN=IN:
                            v.tensor_mul(
                                twr[:, wsl2, :], IN[:, TW_PERM[c], bc(b)],
                                csb[b]),
                            waits=[e_csb[b], twr_free[wsl2],
                                   in_ev.get((b, TW_PERM[c]))], inc="v")
                op2 = "tensor_add" if TW_SIGN[c] > 0 else "tensor_sub"
                ev = p.op("vector", lambda v, b=b, c=c, wsl2=wsl2, IN=IN, OUT=OUT,
                          op2=op2: getattr(v, op2)(
                              OUT[:, c, bc(b)], IN[:, c, bc(b)], twr[:, wsl2, :]),
                          waits=[e_tw, e_hc[(min(c, 1), b)] if c < 2 else None,
                                 e_hb[(c - 4, b)] if 4 <= c < 8 else None,
                                 in_ev.get((b, c))], inc="v")
                twr_free[wsl2] = ev
                out_ev[(b, c)] = ev
        for b in range(B):
            csb_free[b] = out_ev[(b, KC - 1)]
        for b in range(B):
            if cyc == 2:
                for kc in range(KC):
                    j = b * KC + kc
                    osl = j % 2
                    e_f = p.op("vector", lambda v, b=b, kc=kc, osl=osl, OUT=OUT:
                               v.scalar_tensor_tensor(
                                   ostg[:, osl, :], OUT[:, kc, bc(b)],
                                   modp[:, 5, kc, b:b + 1], xbf[:, kc, bc(b)],
                                   ALU.mult, ALU.add),
                               waits=[out_ev[(b, kc)], ostg_free[osl],
                                      e_x2[(b, kc)]], inc="v")
                    q = "gpsimd" if osl == 0 else "sync"
                    e_st = p.op(q, lambda g, b=b, kc=kc, osl=osl: g.dma_start(
                        out=o4[b, kc, :, :], in_=ostg[:, osl, :]),
                        waits=[e_f], inc=f"st{osl}", inc_n=16)
                    ostg_free[osl] = e_st
        cur_src, cur_src_ev = OUT, out_ev

    p.op("gpsimd", lambda g: g.engine_nop(),
         waits=[("st0", p.counts["st0"]), ("st1", p.counts["st1"])])

    if DEBUG_TAPS:
        def dout(name, shape, dtype=F32):
            return nc.dram_tensor(name, shape, dtype, kind="ExternalOutput").ap()
        taps = [
            ("d_modp", modp[:, :, :, :].rearrange("p a b c -> p (a b c)"), [128, 192], F32),
            ("d_pack", pack[:, :, :].rearrange("p a t -> p (a t)"), [16, 3 * NTOK], BF16),
            ("d_vg", agin_s[:, :], [128, 64], F32),
            ("d_h0", h0s[:, :], [128, 32], F32),
            ("d_x2", xbf[:, :, :].rearrange("p a t -> p (a t)"), [128, KC * NTOK], BF16),
            ("d_gf", gfs[:, :], [1, 16], F32),
            ("d_hy", hy[:, :, :].rearrange("p a t -> p (a t)"), [128, KC * NTOK], BF16),
            ("d_pong", pong[:, :, :].rearrange("p a t -> p (a t)"), [128, KC * NTOK], BF16),
            ("d_hsum", hsum[:, :, :].rearrange("p a b -> p (a b)"), [128, 32], F32),
            ("d_gctx", gctxf[:, :, :].rearrange("p a b -> p (a b)"), [128, 32], F32),
        ]
        for name, ap, shape, dt in taps:
            dst = dout(name, shape, dt)
            p.op("gpsimd", lambda g, dst=dst, ap=ap: g.dma_start(out=dst, in_=ap),
                 waits=[("st0", p.counts["st0"]), ("st1", p.counts["st1"])],
                 inc="gd", inc_n=16)
        p.op("gpsimd", lambda g: g.engine_nop(), waits=[("gd", p.counts["gd"])])

    with nc.Block() as block:
        def emit(eng_name):
            def run(eng):
                last = {}
                for waits, fn, inc, target, inc_n in p.ops[eng_name]:
                    for (sem, val) in waits:
                        if val > last.get(sem, -1):
                            eng.wait_ge(sems[sem], val)
                            last[sem] = val
                    instr = fn(eng)
                    if inc is not None:
                        instr.then_inc(sems[inc], inc_n)
            return run

        block.sync(emit("sync"))
        block.gpsimd(emit("gpsimd"))
        block.vector(emit("vector"))
        block.scalar(emit("scalar"))
        block.tensor(emit("tensor"))

    stack.close()
    return nc


_NC_CACHE = {}
_LAST_RES = None


def _get_nc():
    if "nc" not in _NC_CACHE:
        _NC_CACHE["nc"] = build_program()
    return _NC_CACHE["nc"]


def kernel(**inp):
    f32 = np.float32

    def a(name):
        return np.asarray(inp[name], dtype=f32)

    x = a("L0_attn_x")
    wq, wk, wv, wo = a("L0_attn_Wq"), a("L0_attn_Wk"), a("L0_attn_Wv"), a("L0_attn_Wo")
    wbeta = a("L0_attn_Wbeta")
    b_beta = a("L0_attn_b_beta")
    b_o, ln_g, ln_b = a("L0_attn_b_o"), a("L0_attn_ln_g"), a("L0_attn_ln_b")
    cw = {k: a("L0_ffn_" + k) for k in
          ["cn_w1", "cn_b1", "cn_w2", "cn_b2", "cn_w3", "cn_b3", "cn_w4", "cn_b4",
           "gc_w1", "gc_b1", "gc_w2", "gc_b2", "gc_w3", "gc_b3", "adaptive_range"]}
    L0c, wmod = a("L0_c"), a("L0_wmod")

    def tile_w(w):
        K, M = w.shape
        return np.ascontiguousarray(
            w.reshape(K // 128, 128, M // 128, 128).transpose(2, 1, 0, 3))

    def col8(vec, n):
        return np.ascontiguousarray(vec.reshape(n, 128).T)

    wkt = tile_w(wk)
    wot = np.ascontiguousarray(wo.reshape(8, 128, 8, 128))
    cn1t = tile_w(cw["cn_w1"])
    cn2t = tile_w(cw["cn_w2"])
    cn3t = tile_w(cw["cn_w3"])
    cn4 = np.ascontiguousarray(cw["cn_w4"].reshape(2, 128).T)
    gcw1 = tile_w(cw["gc_w1"])
    gcw2 = tile_w(cw["gc_w2"])
    gcw3 = np.ascontiguousarray(cw["gc_w3"].reshape(2, 128).T)
    bias80 = b_beta.reshape(16, 1).astype(f32)
    misc = np.array([[cw["cn_b4"][0], cw["gc_b3"][0],
                      cw["adaptive_range"][0], 0.0]], f32)
    ct = np.ascontiguousarray(L0c.T)
    ones128 = np.ones((128, 128), f32)
    ones1 = np.ones((1, 128), f32)
    selc = np.zeros((16, 1024), f32)
    for c in range(8):
        selc[2 * c, c * 128:c * 128 + 64] = 1.0
        selc[2 * c + 1, c * 128 + 64:(c + 1) * 128] = 1.0
    wmod_sl = wmod.reshape(8, 128, 6144)

    bsel_h = np.zeros((128, 8, 16), f32)
    for qc in range(8):
        for pp in range(128):
            bsel_h[pp, qc, 2 * qc + pp // 64] = 1.0
    shared = {
        "wqT": np.ascontiguousarray(wq.T), "wvT": np.ascontiguousarray(wv.T),
        "bsel": bsel_h, "wkt": wkt, "wot2": wot,
        "cn1t": cn1t, "cn2t": cn2t, "cn3t": cn3t, "cn4": cn4,
        "gcw1t": gcw1, "gcw2t": gcw2, "gcw3": gcw3,
        "wbeta": wbeta, "bbeta": bias80,
        "lng": col8(ln_g, 8), "lnb": col8(ln_b, 8), "bo": col8(b_o, 8),
        "cnb1": col8(cw["cn_b1"], 8), "cnb2": col8(cw["cn_b2"], 4),
        "cnb3": col8(cw["cn_b3"], 2),
        "gcb1": col8(cw["gc_b1"], 4), "gcb2": col8(cw["gc_b2"], 2),
        "misc": misc, "ct": ct, "ident128": np.eye(128, dtype=f32),
        "ones128": ones128, "ones1": ones1, "selc": selc,
    }
    in_maps = []
    for c in range(NCORES):
        xs = x[:, c * TPC:(c + 1) * TPC, :]
        x4 = np.ascontiguousarray(
            xs.reshape(B, TPC, KC, 128).transpose(0, 2, 3, 1))
        onehot = np.zeros((128, 8), f32)
        onehot[:, c] = 1.0
        m = dict(shared)
        m["x4"] = x4
        m["wmod_sl"] = np.ascontiguousarray(wmod_sl[:, :, c * 768:(c + 1) * 768])
        m["onehot"] = onehot
        in_maps.append(m)

    nc = _get_nc()
    res = run_bass_kernel_spmd(nc, in_maps, core_ids=list(range(NCORES)))
    global _LAST_RES
    _LAST_RES = res.results

    out = np.empty((B, S, HID), f32)
    for c in range(NCORES):
        o4 = np.asarray(res.results[c]["o4"])
        out[:, c * TPC:(c + 1) * TPC, :] = o4.transpose(0, 3, 1, 2).reshape(B, TPC, HID)
    return out


# version 40
# speedup vs baseline: 1.0097x; 1.0097x over previous
import contextlib
import numpy as np
import concourse.bass as bass
import concourse.mybir as mybir
from concourse.bass_utils import run_bass_kernel_spmd

F32 = mybir.dt.float32
BF16 = mybir.dt.bfloat16
AF = mybir.ActivationFunctionType
ALU = mybir.AluOpType
AX = mybir.AxisListType

NCORES = 8
B = 4
S = 4096
HID = 1024
NH = 16
TPC = S // NCORES
NTOK = B * TPC
KC = HID // 128
EPS = 1e-5
ENGINES = ("sync", "gpsimd", "vector", "scalar", "tensor")

TW_PERM = [4, 5, 6, 7, 0, 1, 2, 3]
TW_SIGN = [1.0, 1.0, -1.0, -1.0, -1.0, -1.0, 1.0, 1.0]
DEBUG_TAPS = False
DBG_U_MODE = "mul"


class Plan:

    def __init__(self):
        self.ops = {e: [] for e in ENGINES}
        self.counts = dict.fromkeys(
            ["ld", "gd", "st", "cc", "t", "v", "s", "g",
             "sg0", "sg1", "st0", "st1",
             "sh0", "sh1", "sh2", "sh3"], 0)

    def ev(self, sem):
        return (sem, self.counts[sem])

    def op(self, eng, fn, waits=(), inc=None, inc_n=1):
        waits = tuple(w for w in waits if w is not None)
        target = None
        if inc is not None:
            self.counts[inc] += inc_n
            target = self.counts[inc]
        self.ops[eng].append((waits, fn, inc, target, inc_n))
        return (inc, target) if inc else None


def build_program():
    nc = bass.Bass("TRN2", target_bir_lowering=False, debug=False)

    def din(name, shape):
        return nc.dram_tensor(name, shape, F32, kind="ExternalInput").ap()

    x4 = din("x4", [B, KC, 128, TPC])
    wq_d = din("wqT", [HID, HID])
    wv_d = din("wvT", [HID, HID])
    wkt = din("wkt", [8, 128, 8, 128])
    wot = din("wot2", [8, 128, 8, 128])
    cn1t = din("cn1t", [8, 128, 8, 128])
    cn2t = din("cn2t", [4, 128, 8, 128])
    cn3t = din("cn3t", [2, 128, 4, 128])
    cn4_d = din("cn4", [128, 2])
    gcw1t = din("gcw1t", [4, 128, 8, 128])
    gcw2t = din("gcw2t", [2, 128, 4, 128])
    gcw3_d = din("gcw3", [128, 2])
    wbeta_d = din("wbeta", [HID, NH])
    bias80_d = din("bbeta", [16, 1])
    lng_d = din("lng", [128, 8])
    lnb_d = din("lnb", [128, 8])
    bo_d = din("bo", [128, 8])
    cnb1_d = din("cnb1", [128, 8])
    cnb2_d = din("cnb2", [128, 4])
    cnb3_d = din("cnb3", [128, 2])
    gcb1_d = din("gcb1", [128, 4])
    gcb2_d = din("gcb2", [128, 2])
    misc_d = din("misc", [1, 4])
    ct_d = din("ct", [HID, B])
    wmod_d = din("wmod_sl", [8, 128, 768])
    ones128_d = din("ones128", [128, 128])
    ones1_d = din("ones1", [1, 128])
    selc_d = din("selc", [16, 1024])
    onehot_d = din("onehot", [128, 8])
    bsel_d = din("bsel", [128, 8, 16])
    ident_d = din("ident128", [128, 128])

    o4 = nc.dram_tensor("o4", [B, KC, 128, TPC], F32, kind="ExternalOutput").ap()

    warm_in = nc.dram_tensor("warm_in", [128, 4], F32)
    warm_out = nc.dram_tensor("warm_out", [NCORES * 128, 4], F32)
    modin_d = nc.dram_tensor("modin", [768, B], F32)
    modag_d = nc.dram_tensor("modag", [NCORES * 768, B], F32)
    agin_d = [nc.dram_tensor(f"agin{h}", [128, 32], F32) for h in range(2)]
    agout_d = [nc.dram_tensor(f"agout{h}", [NCORES * 128, 32], F32) for h in range(2)]
    arin_d = nc.dram_tensor("arin", [128, 32], F32)
    arout_d = nc.dram_tensor("arout", [128, 32], F32)

    stack = contextlib.ExitStack()

    def sbt(name, shape, dtype):
        return stack.enter_context(nc.sbuf_tensor(name, shape, dtype))

    xbf = sbt("xbf", [128, KC, NTOK], BF16)
    hy = sbt("hy", [128, KC, NTOK], BF16)
    pong = sbt("pong", [128, KC, NTOK], BF16)
    ud = sbt("ud", [128, KC, NTOK], BF16)
    pack = sbt("pack", [16, 3, NTOK], BF16)
    wring = sbt("wring", [128, 4, KC, 128], BF16)
    cn3b = sbt("cn3b", [128, 2, 4, 128], BF16)
    cn4b = sbt("cn4b", [128, 2, 1], BF16)
    stg = sbt("stg", [128, 2, 1024], F32)
    scr = sbt("scr", [128, 3, 512], F32)
    mubf = sbt("mubf", [128, B, 512], BF16)
    rstdbf = sbt("rstdbf", [128, B, 512], BF16)
    xsq = sbt("xsq", [128, 4, 512], BF16)
    nrm = sbt("nrm", [128, 4, 1024], BF16)
    hful = sbt("hful", [128, 2, 512], BF16)
    attno = sbt("attno", [128, 2, 512], BF16)
    twr = sbt("twr", [128, 2, 512], BF16)
    agin_s = sbt("agin_s", [128, 64], F32)
    gath = sbt("gath", [128, NCORES, 64], F32)
    h0s = sbt("h0s", [128, 32], F32)
    hpre = sbt("hpre", [128, 2, 32], F32)
    accp = sbt("accp", [128, 2, 32], F32)
    ptmp = sbt("ptmp", [128, 32], F32)
    hsum = sbt("hsum", [128, KC, B], F32)
    gctxf = sbt("gctxf", [128, KC, B], F32)
    gctxb = sbt("gctxb", [128, KC, B], BF16)
    g1b = sbt("g1b", [128, 4, B], BF16)
    g2b = sbt("g2b", [128, 2, B], BF16)
    gc3b = sbt("gc3b", [128, 2, 1], BF16)
    gfs = sbt("gfs", [1, 16], F32)
    tfrow = sbt("tfrow", [1, 2, 512], BF16)
    modp = sbt("modp", [128, 6, KC, B], F32)
    cts = sbt("cts", [128, KC, B], F32)
    css = sbt("css", [128, KC, B], F32)
    wsmall = sbt("wsmall", [128, KC, 80], BF16)
    bias80 = sbt("bbeta_s", [16, 1], F32)
    lng = sbt("lng_s", [128, KC], F32)
    lnb = sbt("lnb_s", [128, KC], F32)
    bo = sbt("bo_s", [128, KC], F32)
    cnb1 = sbt("cnb1_s", [128, KC], F32)
    cnb2 = sbt("cnb2_s", [128, 4], F32)
    cnb3 = sbt("cnb3_s", [128, 2], F32)
    gcb1 = sbt("gcb1_s", [128, 4], F32)
    gcb2 = sbt("gcb2_s", [128, 2], F32)
    misc = sbt("misc_s", [1, 4], F32)
    ones128 = sbt("ones128_s", [128, 128], BF16)
    ones1 = sbt("ones1_s", [1, 128], BF16)
    selc = sbt("selc_s", [16, 8, 128], BF16)
    onehot = sbt("onehot_s", [128, 8], F32)
    gsb = sbt("gsb", [16, B], BF16)
    lg2 = sbt("lg2", [128, KC, B], F32)
    lb2 = sbt("lb2", [128, KC, B], F32)
    gsbf = sbt("gsbf", [16, B], F32)
    bsel = sbt("bsel_s", [128, 8, 16], F32)
    ident = sbt("ident_s", [128, 16], F32)
    vsb = sbt("vsb", [128, 2, 512], BF16)
    ostg = sbt("ostg", [128, 2, 512], F32)

    pg = [stack.enter_context(nc.psum_tensor(f"pg{i}", [128, 512], F32))
          for i in range(8)]

    sems = {n: stack.enter_context(nc.semaphore("sem_" + n))
            for n in ["ld", "gd", "st", "cc", "t", "v", "s", "g",
                      "sg0", "sg1", "st0", "st1",
                      "sh0", "sh1", "sh2", "sh3"]}

    p = Plan()
    bank_free = [None] * 8
    vsb_free = [None, None]
    twr_free = [None, None]

    def bc(b):
        return slice(b * TPC, (b + 1) * TPC)

    e_wrm = p.op("gpsimd", lambda g: g.dma_start(
        out=warm_in.ap(), in_=ones128_d[:, 0:4]), inc="gd", inc_n=16)
    p.op("gpsimd", lambda g: g.collective_compute(
        "AllGather", ALU.bypass, replica_groups=[list(range(NCORES))],
        ins=[warm_in.ap().opt()], outs=[warm_out.ap().opt()]),
        waits=[e_wrm], inc="cc")
    lde = {}
    small_loads = {
        "bias80": (bias80[:, :], bias80_d),
        "lng": (lng[:, :], lng_d), "lnb": (lnb[:, :], lnb_d),
        "bo": (bo[:, :], bo_d), "cnb1": (cnb1[:, :], cnb1_d),
        "cnb2": (cnb2[:, :], cnb2_d), "cnb3": (cnb3[:, :], cnb3_d),
        "gcb1": (gcb1[:, :], gcb1_d), "gcb2": (gcb2[:, :], gcb2_d),
        "misc": (misc[:, :], misc_d), "onehot": (onehot[:, :], onehot_d),
        "cts": (cts[:, :, :], ct_d.rearrange("(kc p) b -> p kc b", p=128)),
        "bsel": (bsel[:, :, :], bsel_d),
        "ident": (ident[:, :], ident_d[:, 0:16]),
        "cn4": (cn4b[:, :, :], cn4_d),
    }
    del small_loads["cn4"]
    for name, (dst, src) in small_loads.items():
        lde_all = p.op("sync", lambda s_, d=dst, s2=src: s_.dma_start(out=d, in_=s2),
                       inc="ld", inc_n=16)
    for name in small_loads:
        lde[name] = lde_all

    stg_free = [None, None]
    stg_i = [0]

    def stg_slot():
        sl = stg_i[0] % 2
        stg_i[0] += 1
        return sl

    def load_stg(view_fn, src, queue="sync", free_ev_setter=True):
        sl = stg_slot()
        fr = stg_free[sl]
        fr = fr if isinstance(fr, list) else [fr]
        e = p.op(queue, lambda s_, sl=sl: s_.dma_start(out=view_fn(sl), in_=src),
                 waits=fr, inc=f"sg{sl}", inc_n=16)
        return sl, e

    sl, e = load_stg(lambda sl: stg[:, sl, 0:128], ones128_d)
    e_ones128 = p.op("gpsimd", lambda g, sl=sl: g.tensor_copy(
        ones128[:, :], stg[:, sl, 0:128]), waits=[e], inc="g")
    stg_free[sl] = e_ones128
    sl, e = load_stg(lambda sl: stg[0:1, sl, 0:128], ones1_d)
    e_ones1 = p.op("gpsimd", lambda g, sl=sl: g.tensor_copy(
        ones1[:, :], stg[0:1, sl, 0:128]), waits=[e], inc="g")
    stg_free[sl] = e_ones1
    sl, e = load_stg(lambda sl: stg[0:16, sl, 0:1024], selc_d)
    e_selc = p.op("gpsimd", lambda g, sl=sl: g.tensor_copy(
        selc[:, :, :], stg[0:16, sl, 0:1024].rearrange("k (c m) -> k c m", c=8)),
        waits=[e], inc="g")
    stg_free[sl] = e_selc
    sl, e = load_stg(lambda sl: stg[0:128, sl, 0:2], cn4_d)
    e_cn4 = p.op("gpsimd", lambda g, sl=sl: g.tensor_copy(
        cn4b[:, :, :], stg[:, sl, 0:2].rearrange("p (c o) -> p c o", o=1)),
        waits=[e], inc="g")
    stg_free[sl] = e_cn4

    e_wz = p.op("vector", lambda v: v.memset(wsmall[:, :, :], 0.0), inc="v")


    sl, e = load_stg(lambda sl: stg[:, sl, 0:128].rearrange("p (kc n) -> p kc n", kc=8),
                     wbeta_d.rearrange("(kc p) n -> p kc n", p=128))
    e_wbeta = p.op("gpsimd", lambda g, sl=sl: g.tensor_copy(
        wsmall[:, :, 64:80], stg[:, sl, 0:128].rearrange("p (kc n) -> p kc n", kc=8)),
        waits=[e, e_wz], inc="g")
    stg_free[sl] = e_wbeta

    wring_free = [None] * 4
    wring_i = [0]

    def load_weight(src, nkc, queue="sync"):
        wsl = wring_i[0] % 4
        wring_i[0] += 1
        sl, e = load_stg(
            lambda sl: stg[:, sl, 0:nkc * 128].rearrange("p (kc m) -> p kc m", kc=nkc),
            src, queue=queue)
        ev = p.op("gpsimd", lambda g, wsl=wsl, sl=sl, nkc=nkc: g.tensor_copy(
            wring[:, wsl, 0:nkc, :],
            stg[:, sl, 0:nkc * 128].rearrange("p (kc m) -> p kc m", kc=nkc)),
            waits=[e, wring_free[wsl]], inc="g")
        stg_free[sl] = ev
        return wsl, ev

    for mc3 in range(2):
        sl, e = load_stg(
            lambda sl: stg[:, sl, 0:512].rearrange("p (kc m) -> p kc m", kc=4),
            cn3t[mc3, :, :, :])
        e_cn3 = p.op("gpsimd", lambda g, mc3=mc3, sl=sl: g.tensor_copy(
            cn3b[:, mc3, :, :],
            stg[:, sl, 0:512].rearrange("p (kc m) -> p kc m", kc=4)),
            waits=[e], inc="g")
        stg_free[sl] = e_cn3

    e_silu = p.op("scalar", lambda s_: s_.activation(
        css[:, :, :], cts[:, :, :], AF.Silu), waits=[lde["cts"]], inc="s")
    e_modmm = {}
    for kc in range(KC):
        sl, e_ldm = load_stg(lambda sl: stg[:, sl, 0:768], wmod_d[kc, :, :], queue="scalar")
        for mf in range(6):
            ev = p.op("tensor", lambda t, kc=kc, mf=mf, sl=sl: t.matmul(
                pg[mf][:, 0:4],
                stg[:, sl, mf * 128:(mf + 1) * 128],
                css[:, kc, :],
                start=(kc == 0), stop=(kc == KC - 1)),
                waits=[e_ldm, e_silu, bank_free[mf]],
                inc="t")
            e_modmm[mf] = ev
        stg_free[sl] = ev
    e_mev = None
    for mf in range(6):
        e_mev = p.op("scalar", lambda s_, mf=mf: s_.copy(
            scr[:, 0, mf * 4:(mf + 1) * 4], pg[mf][:, 0:4]),
            waits=[e_modmm[mf]], inc="s")
        bank_free[mf] = e_mev

    e_x = {}
    half_free = [stg_free[0], stg_free[0], stg_free[1], stg_free[1]]
    for i, (b, kc) in enumerate([(b, kc) for b in range(B) for kc in range(KC)]):
        h = i % 4
        sl, off = h // 2, (h % 2) * 512
        xq = "sync" if h % 2 == 0 else "scalar"
        e = p.op(xq, lambda s_, sl=sl, off=off, b=b, kc=kc: s_.dma_start(
            out=stg[:, sl, off:off + 512], in_=x4[b, kc, :, :]),
            waits=[half_free[h]] if not isinstance(half_free[h], list)
            else half_free[h], inc=f"sh{h}", inc_n=16)
        e_x[(b, kc)] = p.op("vector", lambda v, b=b, kc=kc, sl=sl, off=off:
                            v.tensor_copy(
            xbf[:, kc, bc(b)], stg[:, sl, off:off + 512]), waits=[e], inc="v")
        half_free[h] = e_x[(b, kc)]
    stg_free[0] = [half_free[0], half_free[1]]
    stg_free[1] = [half_free[2], half_free[3]]


    e_mdma = p.op("gpsimd", lambda g: g.dma_start(
        out=modin_d.ap().rearrange("(mf p) b -> p mf b", p=128),
        in_=scr[:, 0, 0:24].rearrange("p (mf b) -> p mf b", b=B)),
        waits=[e_mev], inc="gd", inc_n=16)
    e_modcc = p.op("gpsimd", lambda g: g.collective_compute(
        "AllGather", ALU.bypass, replica_groups=[list(range(NCORES))],
        ins=[modin_d.ap().opt()], outs=[modag_d.ap().opt()]),
        waits=[e_mdma], inc="cc")
    e_modbk = p.op("gpsimd", lambda g: g.dma_start(
        out=modp[:, :, :, :].rearrange("p kind kc b -> p (kind kc) b"),
        in_=modag_d.ap().rearrange("(kind kc p) b -> p (kind kc) b", kind=6, p=128)),
        waits=[e_modcc], inc="gd", inc_n=16)
    e_scrfree = e_mdma

    e_ca1 = p.op("vector", lambda v: v.tensor_scalar_add(
        modp[:, 1, :, :], modp[:, 1, :, :], 1.0), waits=[e_modbk], inc="v")
    e_cb1 = p.op("vector", lambda v: v.tensor_scalar_add(
        modp[:, 4, :, :], modp[:, 4, :, :], 1.0), waits=[e_modbk], inc="v")
    e_clamp = {}
    for kind in (2, 5):
        def _v(i):
            vws = {1: scr[:, 1, 0:32], 2: scr[:, 1, 64:96], 3: scr[:, 2, 0:32]}
            return vws[i].rearrange("p (kc b) -> p kc b", b=B)
        e_m1 = p.op("vector", lambda v, kind=kind: v.tensor_single_scalar(
            _v(1), modp[:, kind, :, :], 0.0, ALU.is_ge),
            waits=[e_modbk, e_scrfree, e_clamp.get(2)], inc="v")
        e_m2 = p.op("vector", lambda v, kind=kind: v.tensor_scalar_max(
            _v(2), modp[:, kind, :, :], 0.25), waits=[e_modbk, e_clamp.get(2)], inc="v")
        e_m3 = p.op("vector", lambda v, kind=kind: v.tensor_scalar_min(
            _v(3), modp[:, kind, :, :], -0.25), waits=[e_modbk, e_clamp.get(2)], inc="v")
        e_m4 = p.op("vector", lambda v: v.tensor_sub(_v(2), _v(2), _v(3)),
                    waits=[e_m2, e_m3], inc="v")
        e_m5 = p.op("vector", lambda v: v.tensor_mul(_v(1), _v(1), _v(2)),
                    waits=[e_m1, e_m4], inc="v")
        e_clamp[kind] = p.op("vector", lambda v, kind=kind: v.tensor_add(
            modp[:, kind, :, :], _v(1), _v(3)), waits=[e_m5], inc="v")
    e_mod = [e_ca1, e_cb1, e_clamp[2], e_clamp[5]]
    scr_free = [e_scrfree, e_clamp[5], e_clamp[5]]


    xsq_free = [None] * 4
    nrm_free = [None] * 4

    def ln_phase(src, dst, scale_col, bias_col, src_ev, pre=(), accum=None,
                 post=None, dst_free=None, aff_pre=()):
        out_ev = {}
        for b in range(B):
            mm_sum = None
            mm_sq = None
            for kc in range(KC):
                sl = (b * KC + kc) % 4
                sqeng, sqn = ("gpsimd", "g") if b % 2 == 0 else ("vector", "v")
                e_sq = p.op(
                    sqeng, lambda g, b=b, kc=kc, sl=sl: g.tensor_mul(
                        xsq[:, sl, :], src[:, kc, bc(b)], src[:, kc, bc(b)]),
                    waits=[src_ev.get((b, kc)) if src_ev else None,
                           xsq_free[sl], *pre], inc=sqn)
                ev = p.op("tensor", lambda t, b=b, kc=kc: t.matmul(
                    pg[2 * (b % 2)][:, :], ones128[:, :], src[:, kc, bc(b)],
                    start=(kc == 0), stop=(kc == KC - 1)),
                    waits=[bank_free[2 * (b % 2)], e_ones128,
                           src_ev.get((b, kc)) if src_ev else None, *pre],
                    inc="t" if kc == KC - 1 else None)
                if kc == KC - 1:
                    mm_sum = ev
                ev = p.op("tensor", lambda t, b=b, kc=kc, sl=sl: t.matmul(
                    pg[2 * (b % 2) + 1][:, :], ones128[:, :], xsq[:, sl, :],
                    start=(kc == 0), stop=(kc == KC - 1)),
                    waits=[bank_free[2 * (b % 2) + 1], e_sq], inc="t")
                xsq_free[sl] = ev
                if kc == KC - 1:
                    mm_sq = ev
            bs = 2 * (b % 2)
            e_mu = p.op("vector", lambda v, b=b, bs=bs: v.tensor_scalar_mul(
                mubf[:, b, :], pg[bs][:, :], 1.0 / HID),
                waits=[mm_sum], inc="v")
            sc3 = b % 3
            e_msq = p.op("vector", lambda v, b=b, sc3=sc3: v.tensor_mul(
                scr[:, sc3, :], mubf[:, b, :], mubf[:, b, :]),
                waits=[e_mu, scr_free[sc3]], inc="v")
            e_v1 = p.op("vector", lambda v, sc3=sc3, bs=bs: v.scalar_tensor_tensor(
                scr[:, sc3, :], pg[bs + 1][:, :], 1.0 / HID, scr[:, sc3, :],
                ALU.mult, ALU.subtract), waits=[mm_sq, e_msq], inc="v")
            e_v2 = p.op("vector", lambda v, sc3=sc3: v.tensor_scalar_add(
                scr[:, sc3, :], scr[:, sc3, :], EPS), waits=[e_v1], inc="v")
            e_rc = p.op("vector", lambda v, sc3=sc3: v.reciprocal(
                scr[:, sc3, :], scr[:, sc3, :]), waits=[e_v2], inc="v")
            e_rstd = p.op("scalar", lambda s_, b=b, sc3=sc3: s_.activation(
                rstdbf[:, b, :], scr[:, sc3, :], AF.Sqrt), waits=[e_rc], inc="s")
            scr_free[sc3] = e_rstd
            bank_free[bs] = e_mu
            bank_free[bs + 1] = e_v1
            for kc2 in range(0, KC, 2):
                sl = (b * 4 + kc2 // 2) % 4
                e_n1 = p.op("vector", lambda v, b=b, kc2=kc2, sl=sl: v.tensor_sub(
                    nrm[:, sl, :].rearrange("p (a t) -> p a t", a=2),
                    src[:, kc2:kc2 + 2, bc(b)],
                    mubf[:, b, :].rearrange("p (a t) -> p a t", a=1)
                    .broadcast_to([128, 2, TPC])),
                    waits=[e_mu, nrm_free[sl],
                           src_ev.get((b, kc2)) if src_ev else None,
                           src_ev.get((b, kc2 + 1)) if src_ev else None], inc="v")
                e_n2 = p.op("vector", lambda v, b=b, sl=sl: v.tensor_mul(
                    nrm[:, sl, :].rearrange("p (a t) -> p a t", a=2),
                    nrm[:, sl, :].rearrange("p (a t) -> p a t", a=2),
                    rstdbf[:, b, :].rearrange("p (a t) -> p a t", a=1)
                    .broadcast_to([128, 2, TPC])),
                    waits=[e_n1, e_rstd], inc="v")
                for kk in range(2):
                    kc = kc2 + kk
                    acc_ap = accum(kc, b) if accum else None
                    dfree = dst_free(kc, b) if dst_free else None
                    nv = nrm[:, sl, kk * TPC:(kk + 1) * TPC]
                    if acc_ap is not None:
                        e_aff = p.op("scalar", lambda s_, b=b, kc=kc, nv=nv,
                                     a=acc_ap: s_.activation(
                                         dst(kc, b), nv, AF.Identity,
                                         bias=bias_col(kc, b),
                                         scale=scale_col(kc, b),
                                         accum_out=a),
                                     waits=[e_n2, dfree, *aff_pre], inc="s")
                    else:
                        e_aff = p.op("scalar", lambda s_, b=b, kc=kc, nv=nv:
                                     s_.activation(
                                         dst(kc, b), nv, AF.Identity,
                                         bias=bias_col(kc, b),
                                         scale=scale_col(kc, b)),
                                     waits=[e_n2, dfree, *aff_pre], inc="s")
                    out_ev[(b, kc)] = e_aff
                    if post is not None:
                        out_ev[(b, kc)] = post(kc, b, e_aff)
                nrm_free[sl] = out_ev[(b, kc2 + 1)]
        return out_ev

    e_h1 = ln_phase(
        src=xbf, dst=lambda kc, b: hy[:, kc, bc(b)],
        scale_col=lambda kc, b: modp[:, 1, kc, b:b + 1],
        bias_col=lambda kc, b: modp[:, 0, kc, b:b + 1],
        src_ev=e_x, aff_pre=e_mod)

    e_cs = {0: None, 32: None}
    for wd, col0 in ((wq_d, 0), (wv_d, 32)):
        cs_last = {6: None, 7: None}
        for qc in range(KC):
            sl, e_wc = load_stg(lambda sl, wd=wd, qc=qc: stg[:, sl, :],
                                wd[qc * 128:(qc + 1) * 128, :], queue="scalar")
            for half, bank in ((0, 6), (1, 7)):
                cs_last[bank] = p.op("tensor", lambda t, qc=qc, half=half,
                                     bank=bank, sl=sl: t.matmul(
                    pg[bank][0:16, :],
                    bsel[:, qc, :],
                    stg[:, sl, half * 512:(half + 1) * 512],
                    start=(qc == 0), stop=(qc == KC - 1)),
                    waits=[e_wc, lde["bsel"], bank_free[bank]], inc="t")
            stg_free[sl] = cs_last[7]
        e_ev0 = p.op("vector", lambda v: v.tensor_copy(
            scr[0:16, 0, :], pg[6][0:16, :]),
            waits=[cs_last[6], e_scrfree, e_clamp[5], scr_free[0]], inc="v")
        e_ev1 = p.op("vector", lambda v: v.tensor_copy(
            scr[0:16, 1, :], pg[7][0:16, :]),
            waits=[cs_last[7], e_scrfree, e_clamp[5], scr_free[1]], inc="v")
        bank_free[6] = e_ev0
        bank_free[7] = e_ev1
        last = None
        for kc in range(KC):
            bank = 6 + kc % 2
            e_tr = p.op("tensor", lambda t, kc=kc, bank=bank: t.transpose(
                pg[bank][:, 0:16],
                scr[0:16, kc // 4, (kc % 4) * 128:(kc % 4) * 128 + 128],
                ident[0:16, 0:16]),
                waits=[e_ev0 if bank == 6 else e_ev1, lde["ident"],
                       bank_free[bank]], inc="t")
            last = p.op("vector", lambda v, kc=kc, bank=bank, col0=col0:
                        v.tensor_copy(
                            wsmall[:, kc, col0:col0 + 16], pg[bank][:, 0:16]),
                        waits=[e_tr, e_wz], inc="v")
            bank_free[bank] = last
        e_cs[col0] = last
    e_wq_cs = e_cs[0]
    e_wv_cs = e_cs[32]
    scr_free[0] = e_cs[32]
    scr_free[1] = e_cs[32]
    e_wsmall = [e_wq_cs, e_wv_cs, e_wbeta]

    e_pack = {}
    for b in range(B):
        bank = 4 + (b % 2)
        mm_last = None
        for kc in range(KC):
            mm_last = p.op("tensor", lambda t, b=b, kc=kc, bank=bank: t.matmul(
                pg[bank][0:80, :], wsmall[:, kc, :], hy[:, kc, bc(b)],
                start=(kc == 0), stop=(kc == KC - 1)),
                waits=[bank_free[bank], e_h1[(b, kc)], *e_wsmall], inc="t")
        eq = p.op("vector", lambda v, b=b, bank=bank: v.tensor_copy(
            pack[:, 0, bc(b)], pg[bank][0:16, :]), waits=[mm_last], inc="v")
        ev2 = p.op("vector", lambda v, b=b, bank=bank: v.tensor_copy(
            pack[:, 1, bc(b)], pg[bank][32:48, :]), waits=[mm_last], inc="v")
        ez = p.op("scalar", lambda s_, b=b, bank=bank: s_.activation(
            pack[:, 2, bc(b)], pg[bank][64:80, :], AF.Sigmoid,
            bias=bias80[:, 0:1]), waits=[mm_last, lde["bias80"]], inc="s")
        bank_free[bank] = ez
        bank_free[bank] = p.op("vector", lambda v: v.engine_nop(),
                               waits=[eq, ev2, ez], inc="v")
        e_pack[(b, "q")] = eq
        e_pack[(b, "v")] = ev2
        e_pack[(b, "z")] = ez

    e_u = {}
    for c in range(KC):
        wsl, e_wk = load_weight(wkt[c, :, :, :], KC)
        mm_k = {}
        for kcin in range(KC):
            for b in range(B):
                ev = p.op("tensor", lambda t, c=c, kcin=kcin, b=b, wsl=wsl: t.matmul(
                    pg[b][:, :], wring[:, wsl, kcin, :], hy[:, kcin, bc(b)],
                    start=(kcin == 0), stop=(kcin == KC - 1)),
                    waits=[e_wk, bank_free[b], e_h1[(b, kcin)]],
                    inc="t" if kcin == KC - 1 else None)
                if kcin == KC - 1:
                    mm_k[b] = ev
        wring_free[wsl] = p.ev("t")
        for b in range(B):
            vb = 4 + (b % 2)
            vsl = b % 2
            e_vs = p.op("tensor", lambda t, c=c, b=b, vb=vb: t.matmul(
                pg[vb][:, :], selc[:, c, :], pack[:, 1, bc(b)],
                start=True, stop=True),
                waits=[bank_free[vb], e_selc, e_pack[(b, "v")]], inc="t")
            e_ve = p.op("vector", lambda v, vb=vb, vsl=vsl: v.tensor_copy(
                vsb[:, vsl, :], pg[vb][:, :]),
                waits=[e_vs, vsb_free[vsl]], inc="v")
            bank_free[vb] = e_ve
            if DBG_U_MODE == "k":
                e_u[(b, c)] = p.op("vector", lambda v, c=c, b=b: v.tensor_copy(
                    ud[:, c, bc(b)], pg[b][:, :]),
                    waits=[mm_k[b], e_ve], inc="v")
            elif DBG_U_MODE == "vs":
                e_u[(b, c)] = p.op("vector", lambda v, c=c, b=b, vsl=vsl: v.tensor_copy(
                    ud[:, c, bc(b)], vsb[:, vsl, :]),
                    waits=[mm_k[b], e_ve], inc="v")
            else:
                e_u[(b, c)] = p.op("vector", lambda v, c=c, b=b, vsl=vsl: v.tensor_mul(
                    ud[:, c, bc(b)], pg[b][:, :], vsb[:, vsl, :]),
                    waits=[mm_k[b], e_ve], inc="v")
            vsb_free[vsl] = e_u[(b, c)]
            bank_free[b] = e_u[(b, c)]

    dbg_ev = {}

    def dbg_tap(name, ap, shape, dtype, waits):
        if not DEBUG_TAPS:
            return None
        dst = nc.dram_tensor(name, shape, dtype, kind="ExternalOutput").ap()
        ev = p.op("gpsimd", lambda g, dst=dst, ap=ap: g.dma_start(out=dst, in_=ap),
                  waits=waits, inc="gd", inc_n=16)
        dbg_ev[name] = ev
        return ev

    e_tap_u = dbg_tap("d_u", ud[:, :, :].rearrange("p a t -> p (a t)"),
                      [128, KC * NTOK], BF16,
                      [e_u[(b, c)] for b in range(B) for c in range(KC)])

    wo_e = {}
    for oc in range(KC):
        sl, e = load_stg(lambda sl, oc=oc: stg[:, sl, 0:1024], wot[oc, :, :, :]
                         .rearrange("p mc m -> p (mc m)"))
        wo_e[oc] = p.op("gpsimd", lambda g, oc=oc, sl=sl: g.tensor_copy(
            pong[:, oc, 0:1024], stg[:, sl, 0:1024]), waits=[e], inc="g")
        stg_free[sl] = wo_e[oc]

    e_v1col = {}
    for i, (b, c) in enumerate([(b, c) for b in range(B) for c in range(KC)]):
        bank = 4 + (i % 2)
        ssl = i % 2
        e_bb = p.op("tensor", lambda t, b=b, c=c, bank=bank: t.matmul(
            pg[bank][:, :], selc[:, c, :], pack[:, 2, bc(b)],
            start=True, stop=True),
            waits=[bank_free[bank], e_selc, e_pack[(b, "z")]], inc="t")
        e_s1 = p.op("vector", lambda v, b=b, c=c, bank=bank, ssl=ssl:
                    v.tensor_tensor_scan(
                        scr[:, ssl, :], pg[bank][:, :], ud[:, c, bc(b)],
                        0.0, ALU.mult, ALU.add),
                    waits=[e_bb, e_u[(b, c)], scr_free[ssl]], inc="v")
        bank_free[bank] = e_s1
        vcol = (b // 2) * 32 + (b % 2) * KC + c
        e_vc = p.op("vector", lambda v, vcol=vcol, ssl=ssl: v.tensor_copy(
            agin_s[:, vcol:vcol + 1], scr[:, ssl, 511:512]),
            waits=[e_s1], inc="v")
        scr_free[ssl] = e_vc
        e_v1col[(b, c)] = e_vc
    e_g = {}
    e_gz = p.op("vector", lambda v: v.memset(twr[0:16, 1, :], 0.0), inc="v")
    for b in range(B):
        eg1 = p.op("vector", lambda v, b=b: v.tensor_tensor_scan(
            twr[0:16, 0, :], pack[:, 2, bc(b)], twr[0:16, 1, :], 1.0,
            ALU.mult, ALU.add),
            waits=[e_pack[(b, "z")], e_gz, e_g.get(b - 1)], inc="v")
        e_g[b] = p.op("vector", lambda v, b=b: v.tensor_copy(
            gsb[:, b:b + 1], twr[0:16, 0, 511:512]), waits=[eg1], inc="v")
    twr_free[0] = e_g[B - 1]
    twr_free[1] = e_g[B - 1]
    e_h0 = {}
    for half in range(2):
        bs = (0, 1) if half == 0 else (2, 3)
        e_gexp_mm = None
        for b in bs:
            for c in range(KC):
                e_gexp_mm = p.op("tensor", lambda t, b=b, c=c: t.matmul(
                    pg[6][:, (b % 2) * KC + c:(b % 2) * KC + c + 1],
                    selc[:, c, :], gsb[:, b:b + 1],
                    start=True, stop=True),
                    waits=[bank_free[6], e_selc, e_g[b]], inc="t")
        e_gex = p.op("vector", lambda v, half=half: v.tensor_copy(
            agin_s[:, half * 32 + 16:half * 32 + 32], pg[6][:, 0:16]),
            waits=[e_gexp_mm], inc="v")
        bank_free[6] = e_gex

        half_v1 = [e_v1col[(b, c)] for b in bs for c in range(KC)] + [e_gex]
        e_agdma = p.op("gpsimd", lambda g, half=half: g.dma_start(
            out=agin_d[half].ap(), in_=agin_s[:, half * 32:half * 32 + 32]),
            waits=half_v1, inc="gd", inc_n=16)
        e_agcc = p.op("gpsimd", lambda g, half=half: g.collective_compute(
            "AllGather", ALU.bypass, replica_groups=[list(range(NCORES))],
            ins=[agin_d[half].ap().opt()], outs=[agout_d[half].ap().opt()]),
            waits=[e_agdma], inc="cc")
        e_agbk = p.op("gpsimd", lambda g, half=half: g.dma_start(
            out=gath[:, :, half * 32:half * 32 + 32],
            in_=agout_d[half].ap().rearrange("(r p) c -> p r c", p=128)),
            waits=[e_agcc], inc="gd", inc_n=16)
        hp = half % 2
        e_z0 = p.op("vector", lambda v, hp=hp: v.memset(hpre[:, hp, 0:16], 0.0),
                    inc="v")
        e_z1 = p.op("vector", lambda v, hp=hp: v.memset(accp[:, hp, 16:32], 0.0),
                    inc="v")
        e_hp = {0: e_z0}
        e_ac = {-1: e_z1}
        for r in range(NCORES):
            cur, nxt = r % 2, (r + 1) % 2
            e_ac[r] = p.op("vector", lambda v, r=r, cur=cur, nxt=nxt, hp=hp,
                           half=half: v.scalar_tensor_tensor(
                               accp[:, hp, cur * 16:cur * 16 + 16],
                               hpre[:, hp, cur * 16 - cur * 16:16] if False else
                               hpre[:, hp, 0:16] if cur == 0 else
                               hpre[:, hp, 16:32],
                               onehot[:, r:r + 1],
                               accp[:, hp, nxt * 16:nxt * 16 + 16],
                               ALU.mult, ALU.add),
                           waits=[e_hp[r], e_ac.get(r - 1, e_z1), lde["onehot"],
                                  e_agbk], inc="v")
            if r < NCORES - 1:
                e_t = p.op("vector", lambda v, r=r, cur=cur, hp=hp, half=half:
                           v.tensor_mul(
                               ptmp[:, 0:16],
                               gath[:, r, half * 32 + 16:half * 32 + 32],
                               hpre[:, hp, 0:16] if cur == 0 else
                               hpre[:, hp, 16:32]),
                           waits=[e_hp[r], e_agbk], inc="v")
                e_hp[r + 1] = p.op("vector", lambda v, r=r, nxt=nxt, hp=hp,
                                   half=half: v.tensor_add(
                                       hpre[:, hp, 0:16] if nxt == 0 else
                                       hpre[:, hp, 16:32],
                                       ptmp[:, 0:16],
                                       gath[:, r, half * 32:half * 32 + 16]),
                                   waits=[e_t], inc="v")
        e_h0[half] = p.op("vector", lambda v, hp=hp, half=half: v.tensor_copy(
            h0s[:, half * 16:half * 16 + 16],
            accp[:, hp, ((NCORES - 1) % 2) * 16:((NCORES - 1) % 2) * 16 + 16]),
            waits=[e_ac[NCORES - 1]], inc="v")

    e_o = {}
    for i, (b, c) in enumerate([(b, c) for b in range(B) for c in range(KC)]):
        bank = 4 + (i % 2)
        qb = 6 + (i % 2)
        hsl = i % 2
        e_b2 = p.op("tensor", lambda t, b=b, c=c, bank=bank: t.matmul(
            pg[bank][:, :], selc[:, c, :], pack[:, 2, bc(b)],
            start=True, stop=True),
            waits=[bank_free[bank], e_pack[(b, "z")]], inc="t")
        e_q2 = p.op("tensor", lambda t, b=b, c=c, qb=qb: t.matmul(
            pg[qb][:, :], selc[:, c, :], pack[:, 0, bc(b)],
            start=True, stop=True),
            waits=[bank_free[qb], e_pack[(b, "q")]], inc="t")
        e_s2 = p.op("vector", lambda v, b=b, c=c, bank=bank, hsl=hsl:
                    v.tensor_tensor_scan(
                        hful[:, hsl, :], pg[bank][:, :], ud[:, c, bc(b)],
                        h0s[:, b * KC + c:b * KC + c + 1], ALU.mult, ALU.add),
                    waits=[e_b2, e_u[(b, c)], e_h0[b // 2], e_tap_u,
                           e_o.get(((i - 2) // KC, (i - 2) % KC)) if i >= 2 else None],
                    inc="v")
        bank_free[bank] = e_s2
        e_o[(b, c)] = p.op("vector", lambda v, b=b, c=c, qb=qb, hsl=hsl:
                           v.tensor_mul(
                               ud[:, c, bc(b)], hful[:, hsl, :], pg[qb][:, :]),
                           waits=[e_s2, e_q2], inc="v")
        bank_free[qb] = e_o[(b, c)]

    hful_free = [e_o[(B - 1, KC - 2)], e_o[(B - 1, KC - 1)]]
    ytmp = nrm
    ytmp_free = [nrm_free[0], nrm_free[1], nrm_free[2], nrm_free[3]]

    e_y = {}
    for b in range(B):
        for mc in range(KC):
            bank = mc % 4
            mm_o = None
            for oc in range(KC):
                mm_o = p.op("tensor", lambda t, b=b, mc=mc, oc=oc, bank=bank:
                            t.matmul(
                                pg[bank][:, :],
                                pong[:, oc, mc * 128:(mc + 1) * 128],
                                ud[:, oc, bc(b)],
                                start=(oc == 0), stop=(oc == KC - 1)),
                            waits=[wo_e[oc], bank_free[bank], e_o[(b, oc)]],
                            inc="t" if oc == KC - 1 else None)
            ysl = (b * KC + mc) % 4
            e_yt = p.op("scalar", lambda s_, mc=mc, b=b, bank=bank, ysl=ysl:
                        s_.activation(
                            ytmp[:, ysl, 0:TPC], pg[bank][:, :], AF.Identity,
                            bias=bo[:, mc:mc + 1]),
                        waits=[mm_o, lde["bo"], ytmp_free[ysl]], inc="s")
            bank_free[bank] = e_yt
            yeng, yinc = ("vector", "v") if ysl % 2 == 0 else ("gpsimd", "g")
            e_y[(b, mc)] = p.op(yeng, lambda v, mc=mc, b=b, ysl=ysl:
                                v.tensor_add(
                                    hy[:, mc, bc(b)], hy[:, mc, bc(b)],
                                    ytmp[:, ysl, 0:TPC]),
                                waits=[e_yt, e_h1[(b, mc)]], inc=yinc)
            ytmp_free[ysl] = e_y[(b, mc)]

    e_tap_o = dbg_tap("d_o", ud[:, :, :].rearrange("p a t -> p (a t)"),
                      [128, KC * NTOK], BF16,
                      [e_o[(b, c)] for b in range(B) for c in range(KC)])
    e_tap_y = dbg_tap("d_y", hy[:, :, :].rearrange("p a t -> p (a t)"),
                      [128, KC * NTOK], BF16,
                      [e_y[(b, kc)] for b in range(B) for kc in range(KC)])

    for _i in range(4):
        nrm_free[_i] = ytmp_free[_i]
    attno_free = [None, None]
    e_x2 = {}

    def post_attn(kc, b, e_aff):
        sl = (b * KC + kc) % 2
        ev = p.op("vector", lambda v, kc=kc, b=b, sl=sl: v.tensor_add(
            xbf[:, kc, bc(b)], xbf[:, kc, bc(b)], attno[:, sl, :]),
            waits=[e_aff], inc="v")
        attno_free[sl] = ev
        e_x2[(b, kc)] = ev
        return ev

    e_y_src = {(b, kc): e_y[(b, kc)] for b in range(B) for kc in range(KC)}

    e_lg2 = p.op("vector", lambda v: v.tensor_mul(
        lg2[:, :, :], lng[:, :].rearrange("p (kc o) -> p kc o", o=1)
        .broadcast_to([128, KC, B]), modp[:, 2, :, :]),
        waits=[lde["lng"], e_clamp[2]], inc="v")
    e_lb2 = p.op("vector", lambda v: v.tensor_mul(
        lb2[:, :, :], lnb[:, :].rearrange("p (kc o) -> p kc o", o=1)
        .broadcast_to([128, KC, B]), modp[:, 2, :, :]),
        waits=[lde["lnb"], e_clamp[2]], inc="v")

    def attno_dst(kc, b):
        return attno[:, (b * KC + kc) % 2, :]

    ln_phase(
        src=hy, dst=attno_dst,
        scale_col=lambda kc, b: lg2[:, kc, b:b + 1],
        bias_col=lambda kc, b: lb2[:, kc, b:b + 1],
        src_ev=e_y_src, aff_pre=[e_lg2, e_lb2],
        post=post_attn,
        dst_free=lambda kc, b: attno_free[(b * KC + kc) % 2])

    e_h2 = ln_phase(
        src=xbf, dst=lambda kc, b: hy[:, kc, bc(b)],
        scale_col=lambda kc, b: modp[:, 4, kc, b:b + 1],
        bias_col=lambda kc, b: modp[:, 3, kc, b:b + 1],
        src_ev=e_x2, pre=[e_tap_y] if DEBUG_TAPS else [],
        aff_pre=[e_cb1], accum=lambda kc, b: hsum[:, kc, b:b + 1])

    gc1_pre = {}
    for mc in range(4):
        gc1_pre[mc] = load_weight(gcw1t[mc, :, :, :], KC)
    gc2_pre = {}
    for mc2 in range(2):
        gc2_pre[mc2] = load_weight(gcw2t[mc2, :, :, :], 4)
    sl, e = load_stg(lambda sl: stg[0:128, sl, 0:2], gcw3_d)
    e_gc3 = p.op("gpsimd", lambda g, sl=sl: g.tensor_copy(
        gc3b[:, :, :], stg[:, sl, 0:2].rearrange("p (c o) -> p c o", o=1)),
        waits=[e], inc="g")
    stg_free[sl] = e_gc3

    cn1_pre = {}
    for mc in range(4):
        cn1_pre[mc] = load_weight(cn1t[mc, :, :, :], KC)

    all_h2 = [e_h2[(b, kc)] for b in range(B) for kc in range(KC)]
    e_ardma = p.op("gpsimd", lambda g: g.dma_start(
        out=arin_d.ap(), in_=hsum[:, :, :].rearrange("p kc b -> p (kc b)")),
        waits=all_h2, inc="gd", inc_n=16)
    e_arcc = p.op("gpsimd", lambda g: g.collective_compute(
        "AllReduce", ALU.add, replica_groups=[list(range(NCORES))],
        ins=[arin_d.ap().opt()], outs=[arout_d.ap().opt()]),
        waits=[e_ardma], inc="cc")
    e_arbk = p.op("gpsimd", lambda g: g.dma_start(
        out=gctxf[:, :, :].rearrange("p kc b -> p (kc b)"), in_=arout_d.ap()),
        waits=[e_arcc], inc="gd", inc_n=16)
    e_gctx = p.op("vector", lambda v: v.tensor_scalar_mul(
        gctxf[:, :, :], gctxf[:, :, :], 1.0 / S), waits=[e_arbk], inc="v")



    e_tap_h2 = dbg_tap("d_h2", hy[:, :, :].rearrange("p a t -> p (a t)"),
                       [128, KC * NTOK], BF16,
                       [e_h2[(b, kc)] for b in range(B) for kc in range(KC)])

    e_coupconst = []

    def plan_gf():
        e_gcb = p.op("vector", lambda v: v.tensor_copy(
            gctxb[:, :, :], gctxf[:, :, :]), waits=[e_gctx], inc="v")
        e_g1mm = {}
        for mc in range(4):
            wsl, e_l = gc1_pre[mc]
            for kc in range(KC):
                e_g1mm[mc] = p.op("tensor", lambda t, kc=kc, mc=mc, wsl=wsl:
                                  t.matmul(
                    pg[4 + mc][:, 0:4], wring[:, wsl, kc, :], gctxb[:, kc, :],
                    start=(kc == 0), stop=(kc == KC - 1)),
                    waits=[e_l, e_gcb, bank_free[4 + mc]], inc="t")
            wring_free[wsl] = p.ev("t")
        e_g1 = {}
        for mc in range(4):
            e_g1[mc] = p.op("scalar", lambda s_, mc=mc: s_.activation(
                g1b[:, mc, :], pg[4 + mc][:, 0:4], AF.Gelu,
                bias=gcb1[:, mc:mc + 1]), waits=[e_g1mm[mc], lde["gcb1"]], inc="s")
            bank_free[4 + mc] = e_g1[mc]
        e_g2mm = {}
        for mc2 in range(2):
            wsl, e_l = gc2_pre[mc2]
            for kc2 in range(4):
                e_g2mm[mc2] = p.op("tensor", lambda t, kc2=kc2, mc2=mc2, wsl=wsl:
                                   t.matmul(
                    pg[mc2][:, 0:4], wring[:, wsl, kc2, :], g1b[:, kc2, :],
                    start=(kc2 == 0), stop=(kc2 == 3)),
                    waits=[e_l, e_g1[kc2], bank_free[mc2]], inc="t")
            wring_free[wsl] = p.ev("t")
        e_g2 = {}
        for mc2 in range(2):
            e_g2[mc2] = p.op("scalar", lambda s_, mc2=mc2: s_.activation(
                g2b[:, mc2, :], pg[mc2][:, 0:4], AF.Gelu,
                bias=gcb2[:, mc2:mc2 + 1]), waits=[e_g2mm[mc2], lde["gcb2"]], inc="s")
            bank_free[mc2] = e_g2[mc2]
        e_g3mm = None
        for kc3 in range(2):
            e_g3mm = p.op("tensor", lambda t, kc3=kc3: t.matmul(
                pg[2][0:1, 0:4], gc3b[:, kc3, :], g2b[:, kc3, :],
                start=(kc3 == 0), stop=(kc3 == 1)),
                waits=[e_gc3, e_g2[kc3], bank_free[2]], inc="t")
        e_gf = p.op("scalar", lambda s_: s_.activation(
            gfs[0:1, 0:4], pg[2][0:1, 0:4], AF.Sigmoid,
            bias=misc[0:1, 1:2]), waits=[e_g3mm, lde["misc"]], inc="s")
        bank_free[2] = e_gf
        e_cb = p.op("vector", lambda v: v.tensor_scalar(
            gfs[0:1, 4:8], gfs[0:1, 0:4], 1.4, -1.0, ALU.mult, ALU.add),
            waits=[e_gf], inc="v")
        e_cb2 = p.op("vector", lambda v: v.tensor_scalar(
            gfs[0:1, 4:8], gfs[0:1, 4:8], misc[0:1, 2:3], 0.1, ALU.mult, ALU.add),
            waits=[e_cb], inc="v")
        e_sc = p.op("vector", lambda v: v.tensor_scalar_mul(
            gfs[0:1, 8:9], misc[0:1, 2:3], 0.6), waits=[e_gf], inc="v")
        e_coupconst.extend([e_cb2, e_sc])

    cur_src, cur_src_ev = hy, e_h2
    tfrow_free = [None, None]
    ostg_free = [None, None]
    csb_free = [attno_free[0], attno_free[1], hful_free[0], hful_free[1]]
    for cyc in range(3):
        OUT = pong if cyc % 2 == 0 else hy
        IN = cur_src
        in_ev = cur_src_ev
        out_ev = {}
        e_ha = {}
        for mc in range(KC):
            if cyc == 0 and mc < 4:
                wsl, e_w = cn1_pre[mc]
            else:
                wsl, e_w = load_weight(cn1t[mc, :, :, :], KC)
            mm = {}
            for kc in range(KC):
                for b in range(B):
                    hab = b + 4 * (mc % 2)
                    ev = p.op("tensor", lambda t, mc=mc, kc=kc, b=b, wsl=wsl,
                              hab=hab: t.matmul(
                                  pg[hab][:, :], wring[:, wsl, kc, :],
                                  IN[:, kc, bc(b)],
                                  start=(kc == 0), stop=(kc == KC - 1)),
                              waits=[e_w, bank_free[hab], in_ev.get((b, kc))],
                              inc="t" if kc == KC - 1 else None)
                    if kc == KC - 1:
                        mm[b] = ev
            wring_free[wsl] = p.ev("t")
            for b in range(B):
                hab = b + 4 * (mc % 2)
                e_ha[(mc, b)] = p.op("scalar", lambda s_, mc=mc, b=b, hab=hab:
                                     s_.activation(
                                         ud[:, mc, bc(b)], pg[hab][:, :], AF.Gelu,
                                         bias=cnb1[:, mc:mc + 1]),
                                     waits=[mm[b], lde["cnb1"],
                                            e_tap_o if cyc == 0 else None,
                                            ud_free.get((mc, b)) if cyc > 0 else None],
                                     inc="s")
                bank_free[hab] = e_ha[(mc, b)]
        ud_free = {}
        e_hb = {}
        for mc2 in range(4):
            wsl, e_w = load_weight(cn2t[mc2, :, :, :], KC)
            mm = {}
            for kc in range(KC):
                for b in range(B):
                    hbb = b + 4 * (mc2 % 2)
                    ev = p.op("tensor", lambda t, mc2=mc2, kc=kc, b=b, wsl=wsl,
                              hbb=hbb: t.matmul(
                                  pg[hbb][:, :], wring[:, wsl, kc, :],
                                  ud[:, kc, bc(b)],
                                  start=(kc == 0), stop=(kc == KC - 1)),
                              waits=[e_w, bank_free[hbb], e_ha[(kc, b)]],
                              inc="t" if kc == KC - 1 else None)
                    if kc == KC - 1:
                        mm[b] = ev
                    ud_free[(kc, b)] = p.ev("t")
            wring_free[wsl] = p.ev("t")
            for b in range(B):
                hbb = b + 4 * (mc2 % 2)
                e_hb[(mc2, b)] = p.op("scalar", lambda s_, mc2=mc2, b=b, OUT=OUT,
                                      hbb=hbb: s_.activation(
                                          OUT[:, 4 + mc2, bc(b)], pg[hbb][:, :],
                                          AF.Gelu, bias=cnb2[:, mc2:mc2 + 1]),
                                      waits=[mm[b], lde["cnb2"],
                                             e_tap_h2 if OUT is hy else None],
                                      inc="s")
                bank_free[hbb] = e_hb[(mc2, b)]
        e_hc = {}
        for mc3 in range(2):
            mm = {}
            for kc2 in range(4):
                for b in range(B):
                    bank = 4 * mc3 + b
                    ev = p.op("tensor", lambda t, mc3=mc3, kc2=kc2, b=b, OUT=OUT:
                              t.matmul(
                                  pg[4 * mc3 + b][:, :], cn3b[:, mc3, kc2, :],
                                  OUT[:, 4 + kc2, bc(b)],
                                  start=(kc2 == 0), stop=(kc2 == 3)),
                              waits=[e_cn3, bank_free[4 * mc3 + b],
                                     e_hb[(kc2, b)]],
                              inc="t" if kc2 == 3 else None)
                    if kc2 == 3:
                        mm[b] = ev
            for b in range(B):
                e_hc[(mc3, b)] = p.op("scalar", lambda s_, mc3=mc3, b=b, OUT=OUT:
                                      s_.activation(
                                          OUT[:, mc3, bc(b)], pg[4 * mc3 + b][:, :],
                                          AF.Gelu, bias=cnb3[:, mc3:mc3 + 1]),
                                      waits=[mm[b], lde["cnb3"]], inc="s")
                bank_free[4 * mc3 + b] = e_hc[(mc3, b)]
        if cyc == 0:
            plan_gf()
        e_coup = {}
        for b in range(B):
            mm_tf = None
            for kc4 in range(2):
                mm_tf = p.op("tensor", lambda t, b=b, kc4=kc4, OUT=OUT: t.matmul(
                    pg[b][0:1, :], cn4b[:, kc4, :], OUT[:, kc4, bc(b)],
                    start=(kc4 == 0), stop=(kc4 == 1)),
                    waits=[e_cn4, bank_free[b], e_hc[(kc4, b)]], inc="t")
            tsl = b % 2
            e_tf = p.op("scalar", lambda s_, b=b, tsl=tsl: s_.activation(
                tfrow[0:1, tsl, :], pg[b][0:1, :], AF.Sigmoid,
                bias=misc[0:1, 0:1]), waits=[mm_tf, lde["misc"],
                                             tfrow_free[tsl]], inc="s")
            bank_free[b] = e_tf
            e_cr = p.op("vector", lambda v, b=b, tsl=tsl: v.tensor_scalar(
                tfrow[0:1, tsl, :], tfrow[0:1, tsl, :],
                gfs[0:1, 8:9], gfs[0:1, 4 + b:5 + b], ALU.mult, ALU.add),
                waits=[e_tf, *e_coupconst], inc="v")
            cpb = 4 + b
            e_coup[b] = p.op("tensor", lambda t, b=b, tsl=tsl, cpb=cpb: t.matmul(
                pg[cpb][:, :], ones1[:, :], tfrow[0:1, tsl, :],
                start=True, stop=True),
                waits=[e_ones1, e_cr, bank_free[cpb]], inc="t")
            tfrow_free[tsl] = e_coup[b]
        csb = [attno[:, 0, :], attno[:, 1, :], hful[:, 0, :], hful[:, 1, :]]
        e_csb = {}
        for b in range(B):
            e_csb[b] = p.op("scalar", lambda s_, b=b: s_.copy(
                csb[b], pg[4 + b][:, :]),
                waits=[e_coup[b], csb_free[b]], inc="s")
            bank_free[4 + b] = e_csb[b]
        for c in range(KC):
            for b in range(B):
                wsl2 = (c * B + b) % 2
                e_tw = p.op("vector", lambda v, b=b, c=c, wsl2=wsl2, IN=IN:
                            v.tensor_mul(
                                twr[:, wsl2, :], IN[:, TW_PERM[c], bc(b)],
                                csb[b]),
                            waits=[e_csb[b], twr_free[wsl2],
                                   in_ev.get((b, TW_PERM[c]))], inc="v")
                op2 = "tensor_add" if TW_SIGN[c] > 0 else "tensor_sub"
                ev = p.op("vector", lambda v, b=b, c=c, wsl2=wsl2, IN=IN, OUT=OUT,
                          op2=op2: getattr(v, op2)(
                              OUT[:, c, bc(b)], IN[:, c, bc(b)], twr[:, wsl2, :]),
                          waits=[e_tw, e_hc[(min(c, 1), b)] if c < 2 else None,
                                 e_hb[(c - 4, b)] if 4 <= c < 8 else None,
                                 in_ev.get((b, c))], inc="v")
                twr_free[wsl2] = ev
                out_ev[(b, c)] = ev
        for b in range(B):
            csb_free[b] = out_ev[(b, KC - 1)]
        for b in range(B):
            if cyc == 2:
                for kc in range(KC):
                    j = b * KC + kc
                    osl = j % 2
                    e_f = p.op("vector", lambda v, b=b, kc=kc, osl=osl, OUT=OUT:
                               v.scalar_tensor_tensor(
                                   ostg[:, osl, :], OUT[:, kc, bc(b)],
                                   modp[:, 5, kc, b:b + 1], xbf[:, kc, bc(b)],
                                   ALU.mult, ALU.add),
                               waits=[out_ev[(b, kc)], ostg_free[osl],
                                      e_x2[(b, kc)]], inc="v")
                    q = "gpsimd" if osl == 0 else "sync"
                    e_st = p.op(q, lambda g, b=b, kc=kc, osl=osl: g.dma_start(
                        out=o4[b, kc, :, :], in_=ostg[:, osl, :]),
                        waits=[e_f], inc=f"st{osl}", inc_n=16)
                    ostg_free[osl] = e_st
        cur_src, cur_src_ev = OUT, out_ev

    p.op("gpsimd", lambda g: g.engine_nop(),
         waits=[("st0", p.counts["st0"]), ("st1", p.counts["st1"])])

    if DEBUG_TAPS:
        def dout(name, shape, dtype=F32):
            return nc.dram_tensor(name, shape, dtype, kind="ExternalOutput").ap()
        taps = [
            ("d_modp", modp[:, :, :, :].rearrange("p a b c -> p (a b c)"), [128, 192], F32),
            ("d_pack", pack[:, :, :].rearrange("p a t -> p (a t)"), [16, 3 * NTOK], BF16),
            ("d_vg", agin_s[:, :], [128, 64], F32),
            ("d_h0", h0s[:, :], [128, 32], F32),
            ("d_x2", xbf[:, :, :].rearrange("p a t -> p (a t)"), [128, KC * NTOK], BF16),
            ("d_gf", gfs[:, :], [1, 16], F32),
            ("d_hy", hy[:, :, :].rearrange("p a t -> p (a t)"), [128, KC * NTOK], BF16),
            ("d_pong", pong[:, :, :].rearrange("p a t -> p (a t)"), [128, KC * NTOK], BF16),
            ("d_hsum", hsum[:, :, :].rearrange("p a b -> p (a b)"), [128, 32], F32),
            ("d_gctx", gctxf[:, :, :].rearrange("p a b -> p (a b)"), [128, 32], F32),
        ]
        for name, ap, shape, dt in taps:
            dst = dout(name, shape, dt)
            p.op("gpsimd", lambda g, dst=dst, ap=ap: g.dma_start(out=dst, in_=ap),
                 waits=[("st0", p.counts["st0"]), ("st1", p.counts["st1"])],
                 inc="gd", inc_n=16)
        p.op("gpsimd", lambda g: g.engine_nop(), waits=[("gd", p.counts["gd"])])

    with nc.Block() as block:
        def emit(eng_name):
            def run(eng):
                last = {}
                for waits, fn, inc, target, inc_n in p.ops[eng_name]:
                    for (sem, val) in waits:
                        if val > last.get(sem, -1):
                            eng.wait_ge(sems[sem], val)
                            last[sem] = val
                    instr = fn(eng)
                    if inc is not None:
                        instr.then_inc(sems[inc], inc_n)
            return run

        block.sync(emit("sync"))
        block.gpsimd(emit("gpsimd"))
        block.vector(emit("vector"))
        block.scalar(emit("scalar"))
        block.tensor(emit("tensor"))

    stack.close()
    return nc


_NC_CACHE = {}
_LAST_RES = None


def _get_nc():
    if "nc" not in _NC_CACHE:
        _NC_CACHE["nc"] = build_program()
    return _NC_CACHE["nc"]


def kernel(**inp):
    f32 = np.float32

    def a(name):
        return np.asarray(inp[name], dtype=f32)

    x = a("L0_attn_x")
    wq, wk, wv, wo = a("L0_attn_Wq"), a("L0_attn_Wk"), a("L0_attn_Wv"), a("L0_attn_Wo")
    wbeta = a("L0_attn_Wbeta")
    b_beta = a("L0_attn_b_beta")
    b_o, ln_g, ln_b = a("L0_attn_b_o"), a("L0_attn_ln_g"), a("L0_attn_ln_b")
    cw = {k: a("L0_ffn_" + k) for k in
          ["cn_w1", "cn_b1", "cn_w2", "cn_b2", "cn_w3", "cn_b3", "cn_w4", "cn_b4",
           "gc_w1", "gc_b1", "gc_w2", "gc_b2", "gc_w3", "gc_b3", "adaptive_range"]}
    L0c, wmod = a("L0_c"), a("L0_wmod")

    def tile_w(w):
        K, M = w.shape
        return np.ascontiguousarray(
            w.reshape(K // 128, 128, M // 128, 128).transpose(2, 1, 0, 3))

    def col8(vec, n):
        return np.ascontiguousarray(vec.reshape(n, 128).T)

    wkt = tile_w(wk)
    wot = np.ascontiguousarray(wo.reshape(8, 128, 8, 128))
    cn1t = tile_w(cw["cn_w1"])
    cn2t = tile_w(cw["cn_w2"])
    cn3t = tile_w(cw["cn_w3"])
    cn4 = np.ascontiguousarray(cw["cn_w4"].reshape(2, 128).T)
    gcw1 = tile_w(cw["gc_w1"])
    gcw2 = tile_w(cw["gc_w2"])
    gcw3 = np.ascontiguousarray(cw["gc_w3"].reshape(2, 128).T)
    bias80 = b_beta.reshape(16, 1).astype(f32)
    misc = np.array([[cw["cn_b4"][0], cw["gc_b3"][0],
                      cw["adaptive_range"][0], 0.0]], f32)
    ct = np.ascontiguousarray(L0c.T)
    ones128 = np.ones((128, 128), f32)
    ones1 = np.ones((1, 128), f32)
    selc = np.zeros((16, 1024), f32)
    for c in range(8):
        selc[2 * c, c * 128:c * 128 + 64] = 1.0
        selc[2 * c + 1, c * 128 + 64:(c + 1) * 128] = 1.0
    wmod_sl = wmod.reshape(8, 128, 6144)

    bsel_h = np.zeros((128, 8, 16), f32)
    for qc in range(8):
        for pp in range(128):
            bsel_h[pp, qc, 2 * qc + pp // 64] = 1.0
    shared = {
        "wqT": np.ascontiguousarray(wq.T), "wvT": np.ascontiguousarray(wv.T),
        "bsel": bsel_h, "wkt": wkt, "wot2": wot,
        "cn1t": cn1t, "cn2t": cn2t, "cn3t": cn3t, "cn4": cn4,
        "gcw1t": gcw1, "gcw2t": gcw2, "gcw3": gcw3,
        "wbeta": wbeta, "bbeta": bias80,
        "lng": col8(ln_g, 8), "lnb": col8(ln_b, 8), "bo": col8(b_o, 8),
        "cnb1": col8(cw["cn_b1"], 8), "cnb2": col8(cw["cn_b2"], 4),
        "cnb3": col8(cw["cn_b3"], 2),
        "gcb1": col8(cw["gc_b1"], 4), "gcb2": col8(cw["gc_b2"], 2),
        "misc": misc, "ct": ct, "ident128": np.eye(128, dtype=f32),
        "ones128": ones128, "ones1": ones1, "selc": selc,
    }
    in_maps = []
    for c in range(NCORES):
        xs = x[:, c * TPC:(c + 1) * TPC, :]
        x4 = np.ascontiguousarray(
            xs.reshape(B, TPC, KC, 128).transpose(0, 2, 3, 1))
        onehot = np.zeros((128, 8), f32)
        onehot[:, c] = 1.0
        m = dict(shared)
        m["x4"] = x4
        m["wmod_sl"] = np.ascontiguousarray(wmod_sl[:, :, c * 768:(c + 1) * 768])
        m["onehot"] = onehot
        in_maps.append(m)

    nc = _get_nc()
    res = run_bass_kernel_spmd(nc, in_maps, core_ids=list(range(NCORES)))
    global _LAST_RES
    _LAST_RES = res.results

    out = np.empty((B, S, HID), f32)
    for c in range(NCORES):
        o4 = np.asarray(res.results[c]["o4"])
        out[:, c * TPC:(c + 1) * TPC, :] = o4.transpose(0, 3, 1, 2).reshape(B, TPC, HID)
    return out
